# version 05
# baseline (speedup 1.0000x reference)
import math
from contextlib import ExitStack

import numpy as np
import ml_dtypes

import concourse.bass as bass
import concourse.mybir as mybir
from concourse.bass_utils import run_bass_kernel_spmd

F32 = mybir.dt.float32
BF16 = mybir.dt.bfloat16
AF = mybir.ActivationFunctionType
ALU = mybir.AluOpType
AX = mybir.AxisListType

NCORES = 8
TOK = 1024
SEQ = 2048
D = 2048
DC = D // 128
INTER = 5632
IC = INTER // 128
D4 = 8192
D4C = D4 // 128
NQ, NKV, HD = 32, 8, 64
KVF = NKV * HD
NEG = -1.0e9

C_EPS5, C_EPS6, C_128, C_N128, C_HEPS, C_NHEPS = 0, 1, 2, 3, 4, 5

ENGINES = ("sync", "tensor", "vector", "scalar", "gpsimd")

ALL_STAGES = ("l0attn", "l0ffn", "l1attn", "l1ffn")
DBG_STOP = None
AM_STOP = None
RACE_DETECT = False
TS_SAFE = True
LAUNCH_GROUPS = [ALL_STAGES]



def _formB(w):
    K, M = w.shape
    return np.ascontiguousarray(
        w.reshape(K // 128, 128, M // 128, 128).transpose(2, 1, 0, 3).reshape(M // 128, 128, K))


def _cols128(v):
    return np.ascontiguousarray(np.asarray(v).reshape(-1, 128).T)



class Sem:
    __slots__ = ("h", "n")

    def __init__(self, ctx, nc, name):
        self.h = ctx.enter_context(nc.semaphore(name))
        self.n = 0


class KB:
    def __init__(self, nc, ctx):
        self.nc = nc
        self.ctx = ctx
        self.q = {e: [] for e in ENGINES}
        self._sems = {}

    def sem(self, name):
        if name not in self._sems:
            self._sems[name] = Sem(self.ctx, self.nc, name)
        return self._sems[name]

    def raw(self, eng, fn):
        self.q[eng].append(fn)

    def wait(self, eng, sem, val):
        if val > 0:
            self.raw(eng, lambda h, s=sem.h, v=val: h.wait_ge(s, v))

    def op(self, eng, fn, wait=(), inc=None, amt=1):
        for (s, v) in wait:
            self.wait(eng, s, v)
        if inc is not None:
            inc.n += amt
            self.raw(eng, lambda h, fn=fn, s=inc.h, k=amt: fn(h).then_inc(s, k))
        else:
            self.raw(eng, lambda h, fn=fn: fn(h))

    def mark(self, eng, sem, amt=1):
        sem.n += amt
        old = self.q[eng][-1]
        def wrapped(h, old=old, sh=sem.h, k=amt):
            inst = old(h)
            return inst.then_inc(sh, k)
        self.q[eng][-1] = wrapped

    def dma(self, eng, out, in_, sem=None):
        if sem is None:
            sem = self.sem("ld")
        self.op(eng, lambda h, o=out, i=in_: h.dma_start(out=o, in_=i), inc=sem, amt=16)
        return sem.n

    def barrier(self, tag=""):
        s = self.sem("bar")
        s.n += len(ENGINES)
        target = s.n
        for e in ENGINES:
            self.raw(e, lambda h, sh=s.h: h.sem_inc(sh, 1))
        for e in ENGINES:
            self.wait(e, s, target)

    def finalize(self, block):
        def make(ops):
            def run(handle):
                for fn in ops:
                    fn(handle)
            return run
        for e in ENGINES:
            getattr(block, e)(make(self.q[e]))


class WStream:

    def __init__(self, kb, sx, name, width):
        nc = kb.nc
        self.kb = kb
        self.width = width
        self.stage = sx.enter_context(nc.sbuf_tensor(f"{name}_s", [128, width], F32))
        self.bf = [sx.enter_context(nc.sbuf_tensor(f"{name}_b{i}", [128, width], BF16))
                   for i in range(2)]
        self.release_cnt = [0, 0]
        self.cast_cnt = 0
        self.ready_cnt = [0, 0]

    def load(self, i, dram_ap, width=None, nch=None):
        kb = self.kb
        w = width or self.width
        s = i % 2
        ld, ve, pe = kb.sem("ld"), kb.sem("ve"), kb.sem("pe")
        kb.wait("sync", ve, self.cast_cnt)
        dst = self.stage[:, 0:w]
        if nch is not None:
            dst = dst.rearrange("p (c n) -> p c n", c=nch)
        t = kb.dma("sync", dst, dram_ap)
        kb.op("vector", lambda h, d=self.bf[s][:, 0:w], x=self.stage[:, 0:w]:
              h.tensor_copy(d, x), wait=[(ld, t), (pe, self.release_cnt[s])], inc=ve)
        self.cast_cnt = ve.n
        self.ready_cnt[s] = ve.n

    def use(self, i):
        s = i % 2
        self.kb.wait("tensor", self.kb.sem("ve"), self.ready_cnt[s])
        return self.bf[s]

    def release(self, i):
        pe = self.kb.sem("pe")
        self.kb.mark("tensor", pe)
        self.release_cnt[i % 2] = pe.n


class TernStream:

    def __init__(self, kb, sx, st, name, width, thr, nthr):
        nc = kb.nc
        self.kb = kb
        self.st = st
        self.width = width
        self.thr, self.nthr = thr, nthr
        self.stage = sx.enter_context(nc.sbuf_tensor(f"{name}_s", [128, width], F32))
        self.tneg = [sx.enter_context(nc.sbuf_tensor(f"{name}_n{i}", [128, width], BF16))
                     for i in range(2)]
        self.tbf = [sx.enter_context(nc.sbuf_tensor(f"{name}_t{i}", [128, width], BF16))
                    for i in range(2)]
        self.ready = [0, 0]
        self.mm_done = [0, 0]
        self.neg_read = [0, 0]
        self.stage_read = (0, 0)

    def load(self, i, dram_ap):
        kb, st = self.kb, self.st
        s = i % 2
        ld, ve, pe, ge = kb.sem("ld"), kb.sem("ve"), kb.sem("pe"), kb.sem("ge")
        if TS_SAFE:
            kb.wait("sync", ge, ge.n)
            kb.wait("sync", ve, ve.n)
        kb.wait("sync", ge, self.stage_read[0])
        kb.wait("sync", ve, self.stage_read[1])
        t_w = kb.dma("sync", self.stage[:, :], dram_ap)
        kb.op("gpsimd", lambda h, d=self.tneg[s][:, :], w=self.stage[:, :], t=self.nthr[:, :]:
              h.tensor_scalar(d, w, t, None, ALU.is_le),
              wait=[(ld, t_w), (ve, self.neg_read[s])], inc=ge)
        t_ge = ge.n
        kb.op("vector", lambda h, d=self.tbf[s][:, :], w=self.stage[:, :], t=self.thr[:, :],
                     n=self.tneg[s][:, :]:
              h.scalar_tensor_tensor(d, w, t, n, ALU.is_ge, ALU.subtract),
              wait=[(ld, t_w), (ge, t_ge), (pe, self.mm_done[s])], inc=ve)
        self.ready[s] = ve.n
        self.neg_read[s] = ve.n
        self.stage_read = (t_ge, ve.n)

    def use(self, i):
        s = i % 2
        self.kb.wait("tensor", self.kb.sem("ve"), self.ready[s])
        return self.tbf[s]

    def release(self, i):
        pe = self.kb.sem("pe")
        self.kb.mark("tensor", pe)
        self.mm_done[i % 2] = pe.n



class St:
    pass


def alloc_persistent(nc, ctx, st, input_decls):
    st.nc = nc
    st.xs = ctx.enter_context(nc.sbuf_tensor("xs", [128, DC * TOK], F32))
    st.xb = ctx.enter_context(nc.sbuf_tensor("xb", [128, DC * TOK], BF16))
    st.ones_bf = ctx.enter_context(nc.sbuf_tensor("sb_ones_bf", [128, 128], BF16))
    st.ones_f32 = ctx.enter_context(nc.sbuf_tensor("sb_ones_f32", [128, 128], F32))
    st.id_bf = ctx.enter_context(nc.sbuf_tensor("sb_id_bf", [128, 128], BF16))
    st.id_f32 = ctx.enter_context(nc.sbuf_tensor("sb_id_f32", [128, 128], F32))
    st.consts = ctx.enter_context(nc.sbuf_tensor("sb_consts", [128, 8], F32))
    st.stats = ctx.enter_context(nc.sbuf_tensor("stats", [128, 16], F32))
    st._ln_id = 0
    st._uid = [0]
    st.d = {}
    for name, shape, dt in input_decls:
        st.d[name] = nc.dram_tensor(name, list(shape), dt, kind="ExternalInput")


def emit_prologue(kb, st):
    ld, ae = kb.sem("ld"), kb.sem("ae")
    kb.dma("sync", st.xs[:, :].rearrange("p (c t) -> p c t", c=DC),
           st.d["x_in"].ap().rearrange("(c p) t -> p c t", p=128))
    kb.dma("sync", st.consts[:, :], st.d["consts"].ap())
    kb.dma("sync", st.ones_bf[:, :], st.d["ones_bf"].ap())
    kb.dma("sync", st.ones_f32[:, :], st.d["ones_f32"].ap())
    kb.dma("sync", st.id_bf[:, :], st.d["id_bf"].ap())
    t_all = kb.dma("sync", st.id_f32[:, :], st.d["id_f32"].ap())
    for kc in range(DC):
        kb.op("scalar", lambda h, d=st.xb[:, kc * TOK:(kc + 1) * TOK],
                     s=st.xs[:, kc * TOK:(kc + 1) * TOK]: h.copy(d, s),
              wait=[(ld, t_all)] if kc == 0 else [], inc=ae)
    kb.barrier("prologue")


def emit_epilogue(kb, st):
    kb.barrier("prestore")
    so = kb.sem("stdone")
    kb.op("gpsimd", lambda h, o=st.d["x_out"].ap().rearrange("(c p) t -> p c t", p=128),
                 i=st.xs[:, :].rearrange("p (c t) -> p c t", c=DC):
          h.dma_start(out=o, in_=i), inc=so, amt=16)
    kb.wait("gpsimd", so, so.n)



def _ln_stats(kb, st, px, ndc, c0, W, inv, eps_col, bf_src=None, f32_src=None):
    nc = st.nc
    pe, ve, ae = kb.sem("pe"), kb.sem("ve"), kb.sem("ae")
    nbl = (W + 511) // 512
    st._ln_id += 1
    nm = f"ln{st._ln_id}"
    st.m2 = px.enter_context(nc.sbuf_tensor(nm + "_m2", [128, W], F32))
    st.var = px.enter_context(nc.sbuf_tensor(nm + "_var", [128, W], F32))
    st.rstd = px.enter_context(nc.sbuf_tensor(nm + "_rstd", [128, W], F32))
    st.lntmp = px.enter_context(nc.sbuf_tensor(nm + "_tmp", [128, W], F32))
    st.sq = [px.enter_context(nc.sbuf_tensor(nm + f"_sq{i}", [128, W], BF16)) for i in range(2)]
    st.stbf = [px.enter_context(nc.sbuf_tensor(nm + f"_sb{i}", [128, W], BF16)) for i in range(2)]
    psum_sum = px.enter_context(nc.psum_tensor(nm + "_psum", [128, W], F32))
    psum_ssq = px.enter_context(nc.psum_tensor(nm + "_pssq", [128, W], F32))
    for kc in range(ndc):
        pp = kc % 2
        war = [(pe, pe.n)] if kc >= 2 else []
        if bf_src is not None:
            src_bf = bf_src(kc)
            sq_src = src_bf
        else:
            kb.op("scalar", lambda h, d=st.stbf[pp][:, 0:W], s=f32_src(kc):
                  h.copy(d, s), wait=war, inc=ae)
            src_bf = st.stbf[pp][:, 0:W]
            sq_src = f32_src(kc)
            war = []
        kb.op("scalar", lambda h, d=st.sq[pp][:, 0:W], s=sq_src:
              h.square(d, s), wait=war, inc=ae)
        t_ae = ae.n
        for nb in range(nbl):
            w0, w1 = nb * 512, min(W, (nb + 1) * 512)
            kb.op("tensor", lambda h, o=psum_sum[:, w0:w1], l=st.ones_bf[:, :],
                         r=src_bf[:, w0:w1], s0=(kc == 0), s1=(kc == ndc - 1):
                      h.matmul(o, l, r, start=s0, stop=s1),
                  wait=[(ae, t_ae)] if nb == 0 else [])
            kb.op("tensor", lambda h, o=psum_ssq[:, w0:w1], l=st.ones_bf[:, :],
                         r=st.sq[pp][:, w0:w1], s0=(kc == 0), s1=(kc == ndc - 1):
                      h.matmul(o, l, r, start=s0, stop=s1))
        kb.mark("tensor", pe)
    t_pe = pe.n
    kb.op("scalar", lambda h, d=st.m2[:, 0:W], s=psum_sum[:, :]:
          h.activation(d, s, AF.Square, scale=inv), wait=[(pe, t_pe)], inc=ae)
    t_ae = ae.n
    kb.op("vector", lambda h, d=st.var[:, 0:W], s=psum_ssq[:, :], m=st.m2[:, 0:W]:
          h.scalar_tensor_tensor(d, s, inv, m, ALU.mult, ALU.subtract),
          wait=[(ae, t_ae)], inc=ve)
    t_ve = ve.n
    kb.op("scalar", lambda h, d=st.var[:, 0:W], s=st.var[:, 0:W],
                 b=st.consts[:, eps_col:eps_col + 1]:
          h.activation(d, s, AF.Sqrt, bias=b), wait=[(ve, t_ve)], inc=ae)
    t_ae = ae.n
    kb.op("vector", lambda h, d=st.rstd[:, 0:W], s=st.var[:, 0:W]:
          h.reciprocal(d, s), wait=[(ae, t_ae)], inc=ve)
    return psum_sum, ve.n


def emit_postln_stream(kb, st, eps_col=C_EPS5):
    inv = 1.0 / D
    ve, ae = kb.sem("ve"), kb.sem("ae")
    with ExitStack() as px:
        psum_sum, t_rstd = _ln_stats(kb, st, px, DC, 0, TOK, inv, eps_col,
                                     f32_src=lambda kc: st.xs[:, kc * TOK:(kc + 1) * TOK])
        for kc in range(DC):
            dstf = st.xs[:, kc * TOK:(kc + 1) * TOK]
            kb.op("vector", lambda h, d=dstf, s=psum_sum[:, :], x=dstf:
                  h.scalar_tensor_tensor(d, s, -inv, x, ALU.mult, ALU.add),
                  wait=[(ve, t_rstd)] if kc == 0 else [], inc=ve)
            kb.op("vector", lambda h, d=dstf, a=dstf, b=st.rstd[:, :]:
                  h.tensor_mul(d, a, b), inc=ve)
            t_n = kb.sem("ve").n
            kb.op("scalar", lambda h, d=st.xb[:, kc * TOK:(kc + 1) * TOK], s=dstf:
                  h.copy(d, s), wait=[(ve, t_n)], inc=ae)
        kb.barrier("postln")


def emit_ln_to_bf(kb, st, px, ndc, W, dst_bf, eps_col, inv, bf_src=None, f32_src=None,
                  gamma_col=None, beta_col=None):
    ve, ae = kb.sem("ve"), kb.sem("ae")
    psum_sum, t_rstd = _ln_stats(kb, st, px, ndc, 0, W, inv, eps_col,
                                 bf_src=bf_src, f32_src=f32_src)
    for kc in range(ndc):
        srcap = bf_src(kc) if bf_src is not None else f32_src(kc)
        tmp = st.lntmp[:, 0:W]
        kb.op("vector", lambda h, d=tmp, s=psum_sum[:, :], x=srcap:
              h.scalar_tensor_tensor(d, s, -inv, x, ALU.mult, ALU.add),
              wait=[(ve, t_rstd), (ae, ae.n)], inc=ve)
        if gamma_col is None:
            kb.op("vector", lambda h, d=dst_bf(kc), a=tmp, b=st.rstd[:, 0:W]:
                  h.tensor_mul(d, a, b), inc=ve)
        else:
            kb.op("vector", lambda h, d=tmp, a=tmp, b=st.rstd[:, 0:W]:
                  h.tensor_mul(d, a, b), inc=ve)
            t_n = ve.n
            kb.op("scalar", lambda h, d=dst_bf(kc), s=tmp, g=gamma_col(kc), b=beta_col(kc):
                  h.activation(d, s, AF.Identity, bias=b, scale=g),
                  wait=[(ve, t_n)], inc=ae)
    kb.wait("vector", ae, kb.sem("ae").n)



def emit_l0attn(kb, st):
    nc = st.nc
    ld, pe, ve, ae, gd, cc = (kb.sem(x) for x in ("ld", "pe", "ve", "ae", "gd", "cc"))
    kvb = nc.dram_tensor("l0_kvb", [1024, 1024], BF16)
    kvg = nc.dram_tensor("l0_kvg", [2048, 1024], BF16)
    afd = nc.dram_tensor("l0_afd", [D, TOK], BF16)
    with ExitStack() as sx:
        with ExitStack() as p1:
            ws = WStream(kb, p1, "a_w", D)
            wvb = p1.enter_context(nc.sbuf_tensor("a_wv", [128, DC * 512], BF16))
            kown = p1.enter_context(nc.sbuf_tensor("a_kown", [128, 4 * TOK], BF16))
            vown = p1.enter_context(nc.sbuf_tensor("a_vown", [128, 8 * 512], BF16))
            bv = p1.enter_context(nc.sbuf_tensor("a_bv", [128, 512], F32))
            bvrow = p1.enter_context(nc.sbuf_tensor("a_bvrow", [1, 512], F32))
            bk = p1.enter_context(nc.sbuf_tensor("a_bk", [128, 4], F32))
            with ExitStack() as pp:
                psum_k = pp.enter_context(nc.psum_tensor("a_pk", [128, TOK], F32))
                psum_v = pp.enter_context(nc.psum_tensor("a_pv", [128, 512], F32))
                psum_bv = pp.enter_context(nc.psum_tensor("a_pbv", [128, 512], F32))
                wv_gate = 0
                for q in range(4):
                    kb.wait("sync", ve, wv_gate)
                    t_w = kb.dma("sync",
                                 ws.stage[:, :].rearrange("p (c n) -> p c n", c=4),
                                 st.d["l0_wv"].ap().rearrange("(c p) n -> p c n", p=128)
                                 [:, q * 4:(q + 1) * 4, :])
                    kb.op("vector", lambda h, d=wvb[:, q * 2048:(q + 1) * 2048],
                                 s=ws.stage[:, :]: h.tensor_copy(d, s),
                          wait=[(ld, t_w)], inc=ve)
                    wv_gate = ve.n
                ws.cast_cnt = wv_gate
                t_wv = ve.n
                if DBG_STOP == "l0attn_wv":
                    return
                kb.dma("sync", bvrow[:, :], st.d["l0_bv_row"].ap())
                t_bk = kb.dma("sync", bk[:, :], st.d["l0_bk"].ap())
                t_bv = t_bk
                kb.op("tensor", lambda h, o=psum_bv[:, :], l=st.ones_f32[0:1, :], r=bvrow[:, :]:
                      h.matmul(o, l, r, start=True, stop=True), wait=[(ld, t_bv)], inc=pe)
                kb.op("vector", lambda h, d=bv[:, :], s=psum_bv[:, :]:
                      h.tensor_copy(d, s), wait=[(pe, pe.n)], inc=ve)
                t_bvb = ve.n
                if DBG_STOP == "l0attn_bv":
                    return
                kb.wait("tensor", ve, t_wv)
                for tt in range(8):
                    for kc in range(DC):
                        kb.op("tensor", lambda h, o=psum_v[:, :],
                                     l=st.xb[:, kc * TOK + tt * 128: kc * TOK + (tt + 1) * 128],
                                     r=wvb[:, kc * 512:(kc + 1) * 512],
                                     s0=(kc == 0), s1=(kc == DC - 1):
                                  h.matmul(o, l, r, start=s0, stop=s1),
                              wait=[(ve, ve.n)] if tt > 0 else [])
                    kb.mark("tensor", pe)
                    t_pv = pe.n
                    kb.op("vector", lambda h, d=vown[:, tt * 512:(tt + 1) * 512],
                                 s=psum_v[:, :], b=bv[:, :]:
                          h.tensor_add(d, s, b), wait=[(pe, t_pv), (ve, t_bvb)], inc=ve)
                if DBG_STOP == "l0attn_v":
                    return
                for m in range(4):
                    ws.load(m, st.d["l0_wk_t"].ap()[m])
                    wsl = ws.use(m)
                    for nb in range(2):
                        for kc in range(DC):
                            kb.op("tensor", lambda h, o=psum_k[:, nb * 512:(nb + 1) * 512],
                                         l=wsl[:, kc * 128:(kc + 1) * 128],
                                         r=st.xb[:, kc * TOK + nb * 512: kc * TOK + (nb + 1) * 512],
                                         s0=(kc == 0), s1=(kc == DC - 1):
                                      h.matmul(o, l, r, start=s0, stop=s1),
                                  wait=[(ae, ae.n)] if (nb == 0 and kc == 0 and m > 0) else [])
                    ws.release(m)
                    t_pk = pe.n
                    kb.op("scalar", lambda h, d=kown[:, m * TOK:(m + 1) * TOK], s=psum_k[:, :],
                                 b=bk[:, m:m + 1]:
                          h.activation(d, s, AF.Identity, bias=b),
                          wait=[(pe, t_pk), (ld, t_bk)], inc=ae)
            if DBG_STOP == "l0attn_k":
                return
            t_evac = (ae.n, ve.n)
            kb.wait("gpsimd", ae, t_evac[0])
            kb.wait("gpsimd", ve, t_evac[1])
            kb.dma("gpsimd", kvb.ap()[0:512, :].rearrange("(c p) t -> p c t", p=128),
                   kown[:, :].rearrange("p (c t) -> p c t", c=4), sem=gd)
            t_gd = kb.dma("gpsimd",
                          kvb.ap()[512:1024, :].rearrange("r (t h) -> (r t) h", t=2)
                          .rearrange("(tt p) h -> p tt h", p=128),
                          vown[:, :].rearrange("p (tt h) -> p tt h", tt=8), sem=gd)
            kb.wait("gpsimd", gd, t_gd)
            if DBG_STOP == "l0attn_dmaout":
                return
            kb.op("gpsimd", lambda h: h.collective_compute(
                "AllGather", ALU.bypass,
                replica_groups=[[0, 1], [2, 3], [4, 5], [6, 7]],
                ins=[kvb.ap().opt()], outs=[kvg.ap().opt()]), inc=cc)
        if DBG_STOP == "l0attn_p1":
            return
        with ExitStack() as p2:
            kfull = p2.enter_context(nc.sbuf_tensor("a_kfull", [128, 4 * SEQ], BF16))
            vfull = p2.enter_context(nc.sbuf_tensor("a_vfull", [128, DC * 512], BF16))
            atn = p2.enter_context(nc.sbuf_tensor("a_atn", [128, DC * 512], BF16))
            sexp = p2.enter_context(nc.sbuf_tensor("a_sexp", [128, SEQ], BF16))
            qg = p2.enter_context(nc.sbuf_tensor("a_qg", [128, DC * 512], BF16))
            afb = [p2.enter_context(nc.sbuf_tensor(f"a_afb{i}", [128, 512], BF16)) for i in range(2)]
            afb_done = [0, 0]
            nmax = p2.enter_context(nc.sbuf_tensor("a_nmax", [128, 1], F32))
            lsum = p2.enter_context(nc.sbuf_tensor("a_lsum", [128, 1], F32))
            rl = p2.enter_context(nc.sbuf_tensor("a_rl", [128, 1], F32))
            bqq = p2.enter_context(nc.sbuf_tensor("a_bqq", [128, 16], F32))
            ws = WStream(kb, p2, "a_w2", D)
            kb.wait("sync", cc, cc.n)
            for r in range(2):
                kb.dma("sync", kfull[:, :].rearrange("p (c t) -> p c t", c=4)[:, :, r * TOK:(r + 1) * TOK],
                       kvg.ap()[r * 1024: r * 1024 + 512, :].rearrange("(c p) t -> p c t", p=128))
                kb.dma("sync",
                       vfull[:, :].rearrange("p (c v) -> p c v", c=DC)[:, r * 8:(r + 1) * 8, :],
                       kvg.ap()[r * 1024 + 512: (r + 1) * 1024, :]
                       .rearrange("r2 (t h) -> (r2 t) h", t=2)
                       .rearrange("(tt p) h -> p tt h", p=128))
            t_bq = kb.dma("sync", bqq[:, :], st.d["l0_bq"].ap())
            kb.op("vector", lambda h, d=bqq[:, :], s=bqq[:, :]:
                  h.tensor_scalar_mul(d, s, 0.125), wait=[(ld, t_bq)], inc=ve)
            t_bqq = ve.n
            t_kv = kb.sem("ld").n
            with ExitStack() as pp:
                psum_q = pp.enter_context(nc.psum_tensor("a_pq", [128, 512], F32))
                psum_s = pp.enter_context(nc.psum_tensor("a_ps", [128, SEQ], F32))
                psum_t = pp.enter_context(nc.psum_tensor("a_pt", [128, SEQ], BF16))
                psum_o = pp.enter_context(nc.psum_tensor("a_po", [128, 512], F32))
                kb.wait("tensor", ld, t_kv)
                for g in range(2):
                    for m in range(DC):
                        ws.load(m, st.d["l0_wq_t"].ap()[m])
                        wsl = ws.use(m)
                        kb.wait("tensor", ae, ae.n)
                        for kc in range(DC):
                            kb.op("tensor", lambda h, o=psum_q[:, :],
                                         l=wsl[:, kc * 128:(kc + 1) * 128],
                                         r=st.xb[:, kc * TOK + g * 512: kc * TOK + (g + 1) * 512],
                                         s0=(kc == 0), s1=(kc == DC - 1):
                                      h.matmul(o, l, r, start=s0, stop=s1))
                        ws.release(m)
                        t_pq = pe.n
                        kb.op("scalar", lambda h, d=qg[:, m * 512:(m + 1) * 512], s=psum_q[:, :],
                                     b=bqq[:, m:m + 1]:
                              h.activation(d, s, AF.Identity, bias=b, scale=0.125),
                              wait=[(pe, t_pq)] + ([(ve, t_bqq)] if (g == 0 and m == 0) else []),
                              inc=ae)
                    t_qg = ae.n
                    kb.wait("tensor", ae, t_qg)
                    for hi in range(NQ):
                        h_ = HPERM[hi]
                        j = h_ // 4
                        qp = 64 * (hi % 2)
                        qc = hi // 2
                        kvp = 64 * (j % 2)
                        kvc = j // 2
                        assert qp == kvp
                        for qs in range(4):
                            kb.wait("tensor", ae, ae.n)
                            for nb in range(4):
                                kb.op("tensor", lambda h, o=psum_s[:, nb * 512:(nb + 1) * 512],
                                             l=qg[qp:qp + 64, qc * 512 + qs * 128: qc * 512 + (qs + 1) * 128],
                                             r=kfull[kvp:kvp + 64, kvc * SEQ + nb * 512: kvc * SEQ + (nb + 1) * 512]:
                                          h.matmul(o, l, r, start=True, stop=True))
                            kb.mark("tensor", pe)
                            t_sc = pe.n
                            kb.op("vector", lambda h, d=nmax[:, :], s=psum_s[:, :]:
                                  h.reduce_max(d, s, axis=AX.X, negate=True),
                                  wait=[(ve, ve.n), (pe, t_sc)], inc=ve)
                            t_mx = ve.n
                            kb.op("scalar", lambda h, d=sexp[:, :], s=psum_s[:, :],
                                         b=nmax[:, :], a=lsum[:, :]:
                                  h.activation(d, s, AF.Exp, bias=b, accum_out=a),
                                  wait=[(ve, t_mx), (pe, pe.n)], inc=ae)
                            t_ex = ae.n
                            kb.op("vector", lambda h, d=rl[:, :], s=lsum[:, :]:
                                  h.reciprocal(d, s), wait=[(ae, t_ex)], inc=ve)
                            t_rl = ve.n
                            kb.op("vector", lambda h, d=sexp[:, :], s=sexp[:, :], r=rl[:, :]:
                                  h.tensor_scalar_mul(d, s, r), wait=[(ve, t_rl)], inc=ve)
                            t_nrm = ve.n
                            kb.wait("tensor", ve, t_nrm)
                            kb.wait("tensor", ae, ae.n)
                            for kc in range(DC):
                                kb.op("tensor", lambda h, o=psum_t[:, kc * 128:(kc + 1) * 128],
                                             i=sexp[:, kc * 128:(kc + 1) * 128], d=st.id_bf[:, :]:
                                          h.transpose(o, i, d))
                            kb.mark("tensor", pe)
                            t_tr = pe.n
                            kb.op("scalar", lambda h,
                                         d=atn[:, :].rearrange("p (c n) -> p c n", c=DC)[:, :, qs * 128:(qs + 1) * 128],
                                         s=psum_t[:, :].rearrange("p (c n) -> p c n", c=DC):
                                  h.copy(d, s), wait=[(pe, t_tr)], inc=ae)
                        t_stripes = ae.n
                        kb.wait("tensor", ae, t_stripes)
                        for kc in range(DC):
                            kb.op("tensor", lambda h, o=psum_o[0:64, :],
                                         l=vfull[:, kc * 512 + j * 64: kc * 512 + (j + 1) * 64],
                                         r=atn[:, kc * 512:(kc + 1) * 512],
                                         s0=(kc == 0), s1=(kc == DC - 1):
                                      h.matmul(o, l, r, start=s0, stop=s1))
                        kb.mark("tensor", pe)
                        t_av = pe.n
                        s_ = hi % 2
                        kb.op("scalar", lambda h, d=afb[s_][0:64, :], s=psum_o[0:64, :]:
                              h.copy(d, s), wait=[(pe, t_av), (kb.sem("gd"), afb_done[s_])],
                              inc=ae)
                        t_afb = ae.n
                        kb.wait("gpsimd", ae, t_afb)
                        kb.dma("gpsimd",
                               afd.ap()[hi * 64:(hi + 1) * 64, g * 512:(g + 1) * 512],
                               afb[s_][0:64, :], sem=kb.sem("gd"))
                        afb_done[s_] = kb.sem("gd").n
            kb.barrier("l0attn_p2")
        if DBG_STOP == "l0attn_p2":
            return
        with ExitStack() as p3:
            ws = WStream(kb, p3, "a_w3", D)
            af = p3.enter_context(nc.sbuf_tensor("a_af", [128, DC * TOK], BF16))
            bo = p3.enter_context(nc.sbuf_tensor("a_bo", [128, 16], F32))
            t_bo = kb.dma("sync", bo[:, :], st.d["l0_bo"].ap())
            kb.wait("sync", kb.sem("gd"), kb.sem("gd").n)
            t_af = kb.dma("sync", af[:, :].rearrange("p (c t) -> p c t", c=DC),
                          afd.ap().rearrange("(c p) t -> p c t", p=128))
            kb.wait("tensor", ld, t_af)
            with ExitStack() as pp:
                psum_op = pp.enter_context(nc.psum_tensor("a_pop", [128, TOK], F32))
                for m in range(DC):
                    ws.load(m, st.d["l0_wo_t"].ap()[m])
                    wsl = ws.use(m)
                    kb.wait("tensor", ve, ve.n)
                    for nb in range(2):
                        for kc in range(DC):
                            kb.op("tensor", lambda h, o=psum_op[:, nb * 512:(nb + 1) * 512],
                                         l=wsl[:, kc * 128:(kc + 1) * 128],
                                         r=af[:, kc * TOK + nb * 512: kc * TOK + (nb + 1) * 512],
                                         s0=(kc == 0), s1=(kc == DC - 1):
                                      h.matmul(o, l, r, start=s0, stop=s1))
                    ws.release(m)
                    t_pop = pe.n
                    kb.op("vector", lambda h, d=st.xs[:, m * TOK:(m + 1) * TOK],
                                 s=psum_op[:, :], b=bo[:, m:m + 1]:
                          h.scalar_tensor_tensor(d, s, b, d, ALU.add, ALU.add),
                          wait=[(pe, t_pop)] + ([(ld, t_bo)] if m == 0 else []), inc=ve)
            kb.barrier("l0attn_p3")
    emit_postln_stream(kb, st, eps_col=C_EPS5)



def emit_l0ffn(kb, st):
    nc = st.nc
    ld, pe, ve, ae = (kb.sem(x) for x in ("ld", "pe", "ve", "ae"))
    with ExitStack() as sx:
        hbf = sx.enter_context(nc.sbuf_tensor("f_h", [128, IC * TOK], BF16))
        with ExitStack() as p1:
            ws = WStream(kb, p1, "f_w", D)
            gs = p1.enter_context(nc.sbuf_tensor("f_gs", [128, IC], F32))
            us = p1.enter_context(nc.sbuf_tensor("f_us", [128, IC], F32))
            sg = p1.enter_context(nc.sbuf_tensor("f_sg", [128, TOK], F32))
            t_s = kb.dma("sync", gs[:, :], st.d["l0_gs"].ap())
            t_s = kb.dma("sync", us[:, :], st.d["l0_us"].ap())
            with ExitStack() as pp:
                psum_g = pp.enter_context(nc.psum_tensor("f_pg", [128, TOK], F32))
                psum_u = pp.enter_context(nc.psum_tensor("f_pu", [128, TOK], F32))
                kb.wait("scalar", ld, t_s)
                for m in range(IC):
                    ws.load(2 * m, st.d["l0_wg_t"].ap()[m])
                    ws.load(2 * m + 1, st.d["l0_wu_t"].ap()[m])
                    wg = ws.use(2 * m)
                    kb.wait("tensor", ae, ae.n)
                    for nb in range(2):
                        for kc in range(DC):
                            kb.op("tensor", lambda h, o=psum_g[:, nb * 512:(nb + 1) * 512],
                                         l=wg[:, kc * 128:(kc + 1) * 128],
                                         r=st.xb[:, kc * TOK + nb * 512: kc * TOK + (nb + 1) * 512],
                                         s0=(kc == 0), s1=(kc == DC - 1):
                                      h.matmul(o, l, r, start=s0, stop=s1))
                    ws.release(2 * m)
                    t_pg = pe.n
                    wu = ws.use(2 * m + 1)
                    kb.wait("tensor", ve, ve.n)
                    for nb in range(2):
                        for kc in range(DC):
                            kb.op("tensor", lambda h, o=psum_u[:, nb * 512:(nb + 1) * 512],
                                         l=wu[:, kc * 128:(kc + 1) * 128],
                                         r=st.xb[:, kc * TOK + nb * 512: kc * TOK + (nb + 1) * 512],
                                         s0=(kc == 0), s1=(kc == DC - 1):
                                      h.matmul(o, l, r, start=s0, stop=s1))
                    ws.release(2 * m + 1)
                    t_pu = pe.n
                    kb.op("scalar", lambda h, d=sg[:, :], s=psum_g[:, :], g=gs[:, m:m + 1]:
                          h.activation(d, s, AF.Silu, scale=g),
                          wait=[(pe, t_pg), (ve, ve.n)], inc=ae)
                    t_sg = ae.n
                    kb.op("vector", lambda h, d=hbf[:, m * TOK:(m + 1) * TOK],
                                 s=psum_u[:, :], u=us[:, m:m + 1], a=sg[:, :]:
                          h.scalar_tensor_tensor(d, s, u, a, ALU.mult, ALU.mult),
                          wait=[(pe, t_pu), (ae, t_sg)], inc=ve)
            kb.barrier("l0ffn_p1")
        with ExitStack() as p2:
            ds = p2.enter_context(nc.sbuf_tensor("f_ds", [128, 16], F32))
            wstage = p2.enter_context(nc.sbuf_tensor("f_dstage", [128, INTER // 2], F32))
            t_ds = kb.dma("sync", ds[:, :], st.d["l0_ds"].ap())
            wdb = [st.xb[:, 0:INTER], st.xb[:, 8192:8192 + INTER]]
            wdb_done = [0, 0]
            cast_gate = 0
            with ExitStack() as pp:
                psum_d = [pp.enter_context(nc.psum_tensor(f"f_pd{i}", [128, TOK], F32))
                          for i in range(2)]
                evac_done = [0, 0]

                def load_wd(m):
                    nonlocal cast_gate
                    s = m % 2
                    for hf in range(2):
                        kb.wait("sync", ve, cast_gate)
                        t_w = kb.dma("sync", wstage[:, :],
                                     st.d["l0_wd_t"].ap()[m][:, hf * (INTER // 2):(hf + 1) * (INTER // 2)])
                        kb.op("vector", lambda h, d=wdb[s][:, hf * (INTER // 2):(hf + 1) * (INTER // 2)],
                                     w=wstage[:, :]: h.tensor_copy(d, w),
                              wait=[(ld, t_w), (pe, wdb_done[s])], inc=ve)
                        cast_gate = ve.n
                    return ve.n

                ready = [0, 0]
                ready[0] = load_wd(0)
                for m in range(DC):
                    s = m % 2
                    if m + 1 < DC:
                        ready[(m + 1) % 2] = load_wd(m + 1)
                    kb.wait("tensor", ve, ready[s])
                    ps = m % 2
                    kb.wait("tensor", ve, evac_done[ps])
                    for nb in range(2):
                        for kc in range(IC):
                            kb.op("tensor", lambda h, o=psum_d[ps][:, nb * 512:(nb + 1) * 512],
                                         l=wdb[s][:, kc * 128:(kc + 1) * 128],
                                         r=hbf[:, kc * TOK + nb * 512: kc * TOK + (nb + 1) * 512],
                                         s0=(kc == 0), s1=(kc == IC - 1):
                                      h.matmul(o, l, r, start=s0, stop=s1))
                    kb.mark("tensor", pe)
                    wdb_done[s] = pe.n
                    t_pd = pe.n
                    kb.op("vector", lambda h, d=st.xs[:, m * TOK:(m + 1) * TOK],
                                 s=psum_d[ps][:, :], c=ds[:, m:m + 1]:
                          h.scalar_tensor_tensor(d, s, c, d, ALU.mult, ALU.add),
                          wait=[(pe, t_pd)] + ([(ld, t_ds)] if m == 0 else []), inc=ve)
                    evac_done[ps] = ve.n
            kb.barrier("l0ffn_p2")
    emit_postln_stream(kb, st, eps_col=C_EPS5)



def emit_l1attn(kb, st):
    nc = st.nc
    ld, pe, ve, ae, gd, cc = (kb.sem(x) for x in ("ld", "pe", "ve", "ae", "gd", "cc"))
    kvb = nc.dram_tensor("l1_kvb", [6144, 1024], BF16)
    kvg = nc.dram_tensor("l1_kvg", [12288, 1024], BF16)
    scale = 1.0 / math.sqrt(2048.0)
    afd = nc.dram_tensor("l1_afd", [D, TOK], BF16)
    with ExitStack() as sx:
        xn = sx.enter_context(nc.sbuf_tensor("b_xn", [128, DC * TOK], BF16))
        gamma = sx.enter_context(nc.sbuf_tensor("b_gam", [128, 16], F32))
        beta = sx.enter_context(nc.sbuf_tensor("b_bet", [128, 16], F32))
        t_gb = kb.dma("sync", gamma[:, :], st.d["l1_gamma"].ap())
        t_gb = kb.dma("sync", beta[:, :], st.d["l1_beta"].ap())
        kb.wait("scalar", ld, t_gb)
        with ExitStack() as px:
            emit_ln_to_bf(kb, st, px, DC, TOK,
                          dst_bf=lambda kc: xn[:, kc * TOK:(kc + 1) * TOK],
                          eps_col=C_EPS5, inv=1.0 / D,
                          f32_src=lambda kc: st.xs[:, kc * TOK:(kc + 1) * TOK],
                          gamma_col=lambda kc: gamma[:, kc:kc + 1],
                          beta_col=lambda kc: beta[:, kc:kc + 1])
        kb.barrier("l1_ln")
        if DBG_STOP == "l1_ln":
            return
        with ExitStack() as p1:
            ws = WStream(kb, p1, "b_w", D)
            kown = [p1.enter_context(nc.sbuf_tensor(f"b_kown{i}", [128, TOK], BF16))
                    for i in range(2)]
            kown_done = [0, 0]
            with ExitStack() as pp:
                psum_k = pp.enter_context(nc.psum_tensor("b_pk", [128, TOK], F32))
                for m in range(DC):
                    ws.load(m, st.d["l1_wk_t"].ap()[m])
                    wsl = ws.use(m)
                    kb.wait("tensor", ae, ae.n)
                    for nb in range(2):
                        for kc in range(DC):
                            kb.op("tensor", lambda h, o=psum_k[:, nb * 512:(nb + 1) * 512],
                                         l=wsl[:, kc * 128:(kc + 1) * 128],
                                         r=xn[:, kc * TOK + nb * 512: kc * TOK + (nb + 1) * 512],
                                         s0=(kc == 0), s1=(kc == DC - 1):
                                      h.matmul(o, l, r, start=s0, stop=s1))
                    ws.release(m)
                    t_pk = pe.n
                    s = m % 2
                    kb.op("scalar", lambda h, d=kown[s][:, :], p=psum_k[:, :]:
                          h.copy(d, p), wait=[(pe, t_pk), (gd, kown_done[s])], inc=ae)
                    t_ko = ae.n
                    kb.wait("gpsimd", ae, t_ko)
                    kb.dma("gpsimd", kvb.ap()[m * 128:(m + 1) * 128, :], kown[s][:, :], sem=gd)
                    kown_done[s] = gd.n
            if DBG_STOP == "l1_k":
                return
            wsv = WStream(kb, p1, "b_wv", 8 * 512)
            with ExitStack() as pp:
                psum_v = [pp.enter_context(nc.psum_tensor(f"b_pv{t}", [128, 512], F32))
                          for t in range(8)]
                vtmp = [p1.enter_context(nc.sbuf_tensor(f"b_vt{i}", [128, 512], BF16))
                        for i in range(2)]
                vtmp_done = [0, 0]
                for nb in range(4):
                    for hf in range(2):
                        i = nb * 2 + hf
                        wsv.load(i, st.d["l1_wv"].ap()
                                 .rearrange("(c p) n -> p c n", p=128)
                                 [:, hf * 8:(hf + 1) * 8, nb * 512:(nb + 1) * 512],
                                 nch=8)
                        wsl = wsv.use(i)
                        if hf == 0:
                            kb.wait("tensor", ve, ve.n)
                            kb.wait("tensor", ae, ae.n)
                        for kc8 in range(8):
                            kc = hf * 8 + kc8
                            for tt in range(8):
                                kb.op("tensor", lambda h, o=psum_v[tt][:, :],
                                             l=xn[:, kc * TOK + tt * 128: kc * TOK + (tt + 1) * 128],
                                             r=wsl[:, kc8 * 512:(kc8 + 1) * 512],
                                             s0=(kc == 0), s1=(kc == DC - 1):
                                          h.matmul(o, l, r, start=s0, stop=s1))
                        wsv.release(i)
                    t_pv = pe.n
                    for tt in range(8):
                        s = tt % 2
                        kb.op("vector", lambda h, d=vtmp[s][:, :], s_=psum_v[tt][:, :]:
                              h.tensor_copy(d, s_),
                              wait=[(pe, t_pv), (gd, vtmp_done[s])], inc=ve)
                        t_vt = ve.n
                        kb.wait("gpsimd", ve, t_vt)
                        kb.dma("gpsimd",
                               kvb.ap()[2048 + tt * 256: 2048 + (tt + 1) * 256, :]
                               .rearrange("(p two) t -> p (two t)", two=2)
                               [:, nb * 512:(nb + 1) * 512],
                               vtmp[s][:, :], sem=gd)
                        vtmp_done[s] = gd.n
            if DBG_STOP == "l1_v":
                return
            kb.wait("gpsimd", gd, kb.sem("gd").n)
            kb.op("gpsimd", lambda h: h.collective_compute(
                "AllGather", ALU.bypass,
                replica_groups=[[0, 1], [2, 3], [4, 5], [6, 7]],
                ins=[kvb.ap().opt()], outs=[kvg.ap().opt()]), inc=cc)
        with ExitStack() as p2:
            qg = p2.enter_context(nc.sbuf_tensor("b_qg", [128, DC * TOK], BF16))
            atn = p2.enter_context(nc.sbuf_tensor("b_atn", [128, DC * 512], BF16))
            sexp = p2.enter_context(nc.sbuf_tensor("b_sexp", [128, SEQ], BF16))
            mask = [p2.enter_context(nc.sbuf_tensor("b_msk", [128, SEQ], BF16))]
            afb = [p2.enter_context(nc.sbuf_tensor(f"b_afb{i}", [128, 512], BF16)) for i in range(2)]
            afb_done = [0, 0]
            vtb_done = [0, 0, 0, 0]
            nmax = p2.enter_context(nc.sbuf_tensor("b_nmax", [128, 1], F32))
            lsum = p2.enter_context(nc.sbuf_tensor("b_lsum", [128, 1], F32))
            rl = p2.enter_context(nc.sbuf_tensor("b_rl", [128, 1], F32))
            ws = WStream(kb, p2, "b_w2", D)
            kb.wait("sync", cc, cc.n)
            kb.wait("gpsimd", cc, cc.n)
            with ExitStack() as pq:
                psum_q = pq.enter_context(nc.psum_tensor("b_pq", [128, TOK], F32))
                for m in range(DC):
                    ws.load(m, st.d["l1_wq_t"].ap()[m])
                    wsl = ws.use(m)
                    kb.wait("tensor", ae, ae.n)
                    for nb in range(2):
                        for kc in range(DC):
                            kb.op("tensor", lambda h, o=psum_q[:, nb * 512:(nb + 1) * 512],
                                         l=wsl[:, kc * 128:(kc + 1) * 128],
                                         r=xn[:, kc * TOK + nb * 512: kc * TOK + (nb + 1) * 512],
                                         s0=(kc == 0), s1=(kc == DC - 1):
                                      h.matmul(o, l, r, start=s0, stop=s1))
                    ws.release(m)
                    t_pq = pe.n
                    kb.op("scalar", lambda h, d=qg[:, m * TOK:(m + 1) * TOK], s=psum_q[:, :]:
                          h.activation(d, s, AF.Identity, scale=scale), wait=[(pe, t_pq)], inc=ae)
            t_qdone = pe.n
            ktb = [xn[:, 0:8192], xn[:, 8192:16384]]
            ktb_done = [t_qdone, t_qdone]
            vtb = [ws.bf[0][:, 0:1024], ws.bf[0][:, 1024:2048],
                   ws.bf[1][:, 0:1024], ws.bf[1][:, 1024:2048]]
            vtb_done = [t_qdone] * 4
            with ExitStack() as pp:
                psum_s = pp.enter_context(nc.psum_tensor("b_ps", [128, SEQ], F32))
                psum_t = pp.enter_context(nc.psum_tensor("b_pt", [128, SEQ], BF16))
                psum_o = pp.enter_context(nc.psum_tensor("b_po", [128, 512], F32))
                kb.wait("tensor", ae, ae.n)
                for g in range(2):
                    for qs in range(4):
                        qt = g * 4 + qs
                        mk = mask[0]
                        kb.wait("sync", ve, ve.n)
                        t_mk = kb.dma("sync", mk[:, :], st.d["l1_mask"].ap()[qt])
                        kb.wait("tensor", ae, ae.n)
                        for nb in range(4):
                            r = nb // 2
                            s = (qt * 4 + nb) % 2
                            kb.wait("sync", pe, ktb_done[s])
                            t_ld = kb.dma("sync",
                                          ktb[s].rearrange("p (c n) -> p c n", c=DC),
                                          kvg.ap()[r * 6144: r * 6144 + 2048,
                                                   (nb % 2) * 512:(nb % 2 + 1) * 512]
                                          .rearrange("(c p) t -> p c t", p=128))
                            kb.wait("tensor", ld, t_ld)
                            for kc in range(DC):
                                kb.op("tensor", lambda h, o=psum_s[:, nb * 512:(nb + 1) * 512],
                                             l=qg[:, kc * TOK + g * 512 + qs * 128:
                                                  kc * TOK + g * 512 + (qs + 1) * 128],
                                             r=ktb[s][:, kc * 512:(kc + 1) * 512],
                                             s0=(kc == 0), s1=(kc == DC - 1):
                                          h.matmul(o, l, r, start=s0, stop=s1))
                            kb.mark("tensor", pe)
                            ktb_done[s] = pe.n
                        t_sc = pe.n
                        kb.op("vector", lambda h, d=psum_s[:, :], a=psum_s[:, :], b=mk[:, :]:
                              h.tensor_add(d, a, b),
                              wait=[(pe, t_sc), (kb.sem("ld"), t_mk)], inc=ve)
                        t_mskd = ve.n
                        kb.op("vector", lambda h, d=nmax[:, :], s=psum_s[:, :]:
                              h.reduce_max(d, s, axis=AX.X, negate=True),
                              wait=[(ve, t_mskd)], inc=ve)
                        t_mx = ve.n
                        kb.op("scalar", lambda h, d=sexp[:, :], s=psum_s[:, :],
                                     b=nmax[:, :], a=lsum[:, :]:
                              h.activation(d, s, AF.Exp, bias=b, accum_out=a),
                              wait=[(ve, t_mx)], inc=ae)
                        t_ex = ae.n
                        kb.op("vector", lambda h, d=rl[:, :], s=lsum[:, :]:
                              h.reciprocal(d, s), wait=[(ae, t_ex)], inc=ve)
                        t_rl = ve.n
                        kb.op("vector", lambda h, d=sexp[:, :], s=sexp[:, :], r=rl[:, :]:
                              h.tensor_scalar_mul(d, s, r), wait=[(ve, t_rl)], inc=ve)
                        t_nrm = ve.n
                        kb.wait("tensor", ve, t_nrm)
                        kb.wait("tensor", ae, ae.n)
                        for kc in range(DC):
                            kb.op("tensor", lambda h, o=psum_t[:, kc * 128:(kc + 1) * 128],
                                         i=sexp[:, kc * 128:(kc + 1) * 128], d=st.id_bf[:, :]:
                                      h.transpose(o, i, d))
                        kb.mark("tensor", pe)
                        t_tr = pe.n
                        kb.op("scalar", lambda h,
                                     d=atn[:, :].rearrange("p (c n) -> p c n", c=DC)[:, :, qs * 128:(qs + 1) * 128],
                                     s=psum_t[:, :].rearrange("p (c n) -> p c n", c=DC):
                              h.copy(d, s), wait=[(pe, t_tr)], inc=ae)
                    t_stripes = ae.n
                    kb.wait("tensor", ae, t_stripes)
                    for m in range(DC):
                        kb.wait("tensor", ae, ae.n)
                        for r in range(2):
                            j = (m * 2 + r) % 4
                            base = r * 6144 + 2048
                            kb.wait("sync", pe, vtb_done[j])
                            t_ld = kb.dma("sync",
                                          vtb[j][:, :].rearrange("p (c n) -> p c n", c=8),
                                          kvg.ap()[base: base + 2048, :]
                                          .rearrange("(t two) c -> t (two c)", two=2)
                                          [:, m * 128:(m + 1) * 128]
                                          .rearrange("(kc p) h -> p kc h", p=128))
                            kb.wait("tensor", ld, t_ld)
                            for kc8 in range(8):
                                kc = r * 8 + kc8
                                kb.op("tensor", lambda h, o=psum_o[:, :],
                                             l=vtb[j][:, kc8 * 128:(kc8 + 1) * 128],
                                             r2=atn[:, kc * 512:(kc + 1) * 512],
                                             s0=(kc == 0), s1=(kc == DC - 1):
                                          h.matmul(o, l, r2, start=s0, stop=s1))
                            kb.mark("tensor", pe)
                            vtb_done[j] = pe.n
                        t_av = pe.n
                        s_ = m % 2
                        kb.op("scalar", lambda h, d=afb[s_][:, :], s=psum_o[:, :]:
                              h.copy(d, s), wait=[(pe, t_av), (kb.sem("gd"), afb_done[s_])],
                              inc=ae)
                        t_afb = ae.n
                        kb.wait("gpsimd", ae, t_afb)
                        kb.dma("gpsimd", afd.ap()[m * 128:(m + 1) * 128, g * 512:(g + 1) * 512],
                               afb[s_][:, :], sem=gd)
                        afb_done[s_] = gd.n
            kb.barrier("l1attn_p2")
        if DBG_STOP == "l1_p2":
            return
        with ExitStack() as p3:
            ws = WStream(kb, p3, "b_w3", D)
            af = p3.enter_context(nc.sbuf_tensor("b_af", [128, DC * TOK], BF16))
            kb.wait("sync", gd, gd.n)
            t_af = kb.dma("sync", af[:, :].rearrange("p (c t) -> p c t", c=DC),
                          afd.ap().rearrange("(c p) t -> p c t", p=128))
            kb.wait("tensor", ld, t_af)
            with ExitStack() as pp:
                psum_op = pp.enter_context(nc.psum_tensor("b_pop", [128, TOK], F32))
                for m in range(DC):
                    ws.load(m, st.d["l1_wo_t"].ap()[m])
                    wsl = ws.use(m)
                    kb.wait("tensor", ve, ve.n)
                    for nb in range(2):
                        for kc in range(DC):
                            kb.op("tensor", lambda h, o=psum_op[:, nb * 512:(nb + 1) * 512],
                                         l=wsl[:, kc * 128:(kc + 1) * 128],
                                         r=af[:, kc * TOK + nb * 512: kc * TOK + (nb + 1) * 512],
                                         s0=(kc == 0), s1=(kc == DC - 1):
                                      h.matmul(o, l, r, start=s0, stop=s1))
                    ws.release(m)
                    t_pop = pe.n
                    kb.op("vector", lambda h, d=st.xs[:, m * TOK:(m + 1) * TOK],
                                 s=st.xs[:, m * TOK:(m + 1) * TOK], o=psum_op[:, :]:
                          h.scalar_tensor_tensor(d, s, 2.0, o, ALU.mult, ALU.add),
                          wait=[(pe, t_pop)], inc=ve)
            kb.barrier("l1attn_p3")
    emit_postln_stream(kb, st, eps_col=C_EPS5)



def emit_absmax_clamp(kb, st, px, xn_bf, ndc, c0, W):
    nc = st.nc
    pe, ve, ae = kb.sem("pe"), kb.sem("ve"), kb.sem("ae")
    st._uid[0] += 1
    am = f"am{st._uid[0]}"
    mx = px.enter_context(nc.sbuf_tensor(am + "_mx", [128, W], BF16))
    mn = px.enter_context(nc.sbuf_tensor(am + "_mn", [128, W], BF16))
    grow = px.enter_context(nc.sbuf_tensor(am + "_grow", [1, W], F32))
    rrow = px.enter_context(nc.sbuf_tensor(am + "_rrow", [1, W], F32))
    sb = px.enter_context(nc.sbuf_tensor(am + "_sb", [128, W], F32))
    gb = px.enter_context(nc.sbuf_tensor(am + "_gb", [128, W], F32))
    hi = px.enter_context(nc.sbuf_tensor(am + "_hi", [128, 1], F32))
    lo = px.enter_context(nc.sbuf_tensor(am + "_lo", [128, 1], F32))
    u = px.enter_context(nc.sbuf_tensor(am + "_u", [128, W], F32))
    with ExitStack() as pp:
        ptr = pp.enter_context(nc.psum_tensor(am + "_ptr", [128, 128], BF16))
        prow = [pp.enter_context(nc.psum_tensor(am + f"_prow{j}", [1, 128], F32))
                for j in range(W // 128)]
        pbc = pp.enter_context(nc.psum_tensor(am + "_pbc", [128, W], F32))
        kb.op("vector", lambda h, d=mx[:, :], a=xn_bf(0), b=xn_bf(1):
              h.tensor_tensor(d, a, b, ALU.max), inc=ve)
        kb.op("vector", lambda h, d=mn[:, :], a=xn_bf(0), b=xn_bf(1):
              h.tensor_tensor(d, a, b, ALU.min), inc=ve)
        for kc in range(2, ndc):
            kb.op("vector", lambda h, d=mx[:, :], a=mx[:, :], b=xn_bf(kc):
                  h.tensor_tensor(d, a, b, ALU.max), inc=ve)
            kb.op("vector", lambda h, d=mn[:, :], a=mn[:, :], b=xn_bf(kc):
                  h.tensor_tensor(d, a, b, ALU.min), inc=ve)
        kb.op("vector", lambda h, d=mx[:, :], a=mn[:, :], b=mx[:, :]:
              h.scalar_tensor_tensor(d, a, -1.0, b, ALU.mult, ALU.max), inc=ve)
        if AM_STOP == "chain":
            kb.barrier("am")
            return
        nj = W // 128
        t_mx = ve.n
        kb.wait("tensor", ve, t_mx)
        for j in range(nj):
            kb.op("tensor", lambda h, o=ptr[:, :],
                         i=mx[:, j * 128:(j + 1) * 128], d=st.id_bf[:, :]:
                      h.transpose(o, i, d), wait=[(ve, ve.n)], inc=pe)
            t_tr = pe.n
            kb.op("vector", lambda h, d=st.stats[:, j:j + 1], s=ptr[:, :]:
                  h.reduce_max(d, s, axis=AX.X), wait=[(pe, t_tr)], inc=ve)
        t_gtm = ve.n
        if AM_STOP == "tr":
            kb.barrier("am")
            return
        kb.wait("tensor", ve, t_gtm)
        for j in range(nj):
            kb.op("tensor", lambda h, o=prow[j][:, :],
                         i=st.stats[:, j:j + 1], d=st.id_f32[:, :]:
                      h.transpose(o, i, d))
        kb.mark("tensor", pe)
        t_row = pe.n
        for j in range(nj):
            kb.op("scalar", lambda h, d=grow[:, j * 128:(j + 1) * 128], s=prow[j][:, :]:
                  h.copy(d, s), wait=[(pe, t_row)] if j == 0 else [], inc=ae)
        t_grow = ae.n
        kb.op("vector", lambda h, d=rrow[:, :], s=grow[:, :]:
              h.reciprocal(d, s), wait=[(ae, t_grow)], inc=ve)
        t_rows = (ae.n, ve.n)
        if AM_STOP == "row":
            kb.barrier("am")
            return
        kb.op("tensor", lambda h, o=pbc[:, :], l=st.ones_f32[0:1, :], r=rrow[:, :]:
              h.matmul(o, l, r, start=True, stop=True),
              wait=[(ve, t_rows[1])], inc=pe)
        t_b1 = pe.n
        kb.op("scalar", lambda h, d=sb[:, :], s=pbc[:, :]:
              h.activation(d, s, AF.Identity, scale=128.0), wait=[(pe, t_b1)], inc=ae)
        t_sb = ae.n
        kb.op("tensor", lambda h, o=pbc[:, :], l=st.ones_f32[0:1, :], r=grow[:, :]:
              h.matmul(o, l, r, start=True, stop=True),
              wait=[(ae, max(t_rows[0], t_sb))], inc=pe)
        t_b2 = pe.n
        kb.op("scalar", lambda h, d=gb[:, :], s=pbc[:, :]:
              h.activation(d, s, AF.Identity, scale=1.0 / 128.0), wait=[(pe, t_b2)], inc=ae)
        kb.op("scalar", lambda h, d=hi[:, :], s=st.consts[:, C_EPS6:C_EPS6 + 1],
                     b=st.consts[:, C_128:C_128 + 1]:
              h.activation(d, s, AF.Identity, bias=b, scale=-1.0), inc=ae)
        kb.op("scalar", lambda h, d=lo[:, :], s=st.consts[:, C_EPS6:C_EPS6 + 1],
                     b=st.consts[:, C_N128:C_N128 + 1]:
              h.activation(d, s, AF.Identity, bias=b, scale=1.0), inc=ae)
        t_all = ae.n
        if AM_STOP == "bc":
            kb.barrier("am")
            return
        for kc in range(ndc):
            kb.op("vector", lambda h, d=u[:, :], a=xn_bf(kc), b=sb[:, :]:
                  h.tensor_mul(d, a, b), wait=[(ae, t_all)] if kc == 0 else [], inc=ve)
            kb.op("vector", lambda h, d=u[:, :], s=u[:, :], s1=hi[:, :], s2=lo[:, :]:
                  h.tensor_scalar(d, s, s1, s2, ALU.min, ALU.max), inc=ve)
            kb.op("vector", lambda h, d=xn_bf(kc), a=u[:, :], b=gb[:, :]:
                  h.tensor_mul(d, a, b), inc=ve)
    kb.barrier("absmax")


def _tern_gamma_pass(kb, st, px):
    nc = st.nc
    ld, pe, ve, ae = (kb.sem(x) for x in ("ld", "pe", "ve", "ae"))
    gacc = px.enter_context(nc.sbuf_tensor("t_gacc", [128, D4C + DC], F32))
    gcol = px.enter_context(nc.sbuf_tensor("t_gcol", [128, 2], F32))
    st.gfc = px.enter_context(nc.sbuf_tensor("t_gfc", [128, 1], F32))
    st.gpr = px.enter_context(nc.sbuf_tensor("t_gpr", [128, 1], F32))
    st.thrf = px.enter_context(nc.sbuf_tensor("t_thrf", [128, 1], F32))
    st.nthrf = px.enter_context(nc.sbuf_tensor("t_nthrf", [128, 1], F32))
    st.thrp = px.enter_context(nc.sbuf_tensor("t_thrp", [128, 1], F32))
    st.nthrp = px.enter_context(nc.sbuf_tensor("t_nthrp", [128, 1], F32))
    with ExitStack() as sx:
        wstage = [sx.enter_context(nc.sbuf_tensor(f"t_ws{i}", [128, D4], F32)) for i in range(2)]
        rel = [0, 0]
        def feed(idx, col, dram_ap, width):
            s = idx % 2
            kb.wait("sync", ve, rel[s])
            t = kb.dma("sync", wstage[s][:, 0:width], dram_ap)
            kb.op("vector", lambda h, d=gacc[:, col:col + 1], x=wstage[s][:, 0:width]:
                  h.reduce_sum(d, x, axis=AX.X, apply_absolute_value=True),
                  wait=[(ld, t)], inc=ve)
            rel[s] = ve.n
        idx = 0
        for m in range(D4C):
            feed(idx, m, st.d["l1_wfc_t"].ap()[m], D); idx += 1
        for m in range(DC):
            feed(idx, D4C + m, st.d["l1_wpr_t"].ap()[m], D4); idx += 1
        kb.op("vector", lambda h, d=gcol[:, 0:1], s=gacc[:, 0:D4C]:
              h.reduce_sum(d, s, axis=AX.X), inc=ve)
        kb.op("vector", lambda h, d=gcol[:, 1:2], s=gacc[:, D4C:D4C + DC]:
              h.reduce_sum(d, s, axis=AX.X), inc=ve)
        t_gc = ve.n
        with ExitStack() as pp:
            pg = pp.enter_context(nc.psum_tensor("t_pg", [128, 2], F32))
            kb.op("tensor", lambda h, o=pg[:, :], l=st.ones_f32[:, :], r=gcol[:, :]:
                  h.matmul(o, l, r, start=True, stop=True), wait=[(ve, t_gc)], inc=pe)
            t_pg = pe.n
            inv_n = 1.0 / (D4 * D)
            kb.op("scalar", lambda h, d=st.gfc[:, :], s=pg[:, 0:1]:
                  h.activation(d, s, AF.Identity, scale=inv_n), wait=[(pe, t_pg)], inc=ae)
            kb.op("scalar", lambda h, d=st.gpr[:, :], s=pg[:, 1:2]:
                  h.activation(d, s, AF.Identity, scale=inv_n), inc=ae)
            for gsrc, tdst, ndst in ((st.gfc, st.thrf, st.nthrf), (st.gpr, st.thrp, st.nthrp)):
                kb.op("scalar", lambda h, d=tdst[:, :], s=gsrc[:, :],
                             b=st.consts[:, C_HEPS:C_HEPS + 1]:
                      h.activation(d, s, AF.Identity, bias=b, scale=0.5), inc=ae)
                kb.op("scalar", lambda h, d=ndst[:, :], s=gsrc[:, :],
                             b=st.consts[:, C_NHEPS:C_NHEPS + 1]:
                      h.activation(d, s, AF.Identity, bias=b, scale=-0.5), inc=ae)
    kb.barrier("terngamma")


def emit_l1ffn(kb, st):
    nc = st.nc
    ld, pe, ve, ae = (kb.sem(x) for x in ("ld", "pe", "ve", "ae"))
    with ExitStack() as gx:
        _tern_gamma_pass(kb, st, gx)
        if DBG_STOP == "l1f_gamma":
            return
        for half in range(2):
            c0 = half * 512
            W = 512
            with ExitStack() as sx:
                hbf = sx.enter_context(nc.sbuf_tensor("c_h" + str(half), [128, D4C * W], BF16))
                sxa = sx.enter_context(ExitStack())
                xnf = sxa.enter_context(nc.sbuf_tensor("c_xn" + str(half), [128, DC * W], BF16))
                with ExitStack() as px:
                    emit_ln_to_bf(kb, st, px, DC, W,
                                  dst_bf=lambda kc: xnf[:, kc * W:(kc + 1) * W],
                                  eps_col=C_EPS6, inv=1.0 / D,
                                  f32_src=lambda kc: st.xs[:, kc * TOK + c0: kc * TOK + c0 + W])
                if DBG_STOP == "l1f_ln":
                    return
                with ExitStack() as px:
                    emit_absmax_clamp(kb, st, px,
                                      lambda kc: xnf[:, kc * W:(kc + 1) * W], DC, c0, W)
                if DBG_STOP == "l1f_am":
                    return
                with ExitStack() as pp:
                    psum_f = [pp.enter_context(nc.psum_tensor(f"c_pf{half}{i}", [128, W], F32))
                              for i in range(2)]
                    tsf = TernStream(kb, pp, st, "c_tsf" + str(half), D, st.thrf, st.nthrf)
                    tsf.load(0, st.d["l1_wfc_t"].ap()[0])
                    gelu_done = [0, 0]
                    for oc in range(D4C):
                        if oc + 1 < D4C:
                            tsf.load(oc + 1, st.d["l1_wfc_t"].ap()[oc + 1])
                        wsl = tsf.use(oc)
                        ps = oc % 2
                        kb.wait("tensor", ae, gelu_done[ps])
                        for kc in range(DC):
                            kb.op("tensor", lambda h, o=psum_f[ps][:, :],
                                         l=wsl[:, kc * 128:(kc + 1) * 128],
                                         r=xnf[:, kc * W:(kc + 1) * W],
                                         s0=(kc == 0), s1=(kc == DC - 1):
                                      h.matmul(o, l, r, start=s0, stop=s1))
                        tsf.release(oc)
                        t_pf = pe.n
                        kb.op("scalar", lambda h, d=hbf[:, oc * W:(oc + 1) * W],
                                     s=psum_f[ps][:, :], g=st.gfc[:, :]:
                              h.activation(d, s, AF.Gelu, scale=g), wait=[(pe, t_pf)], inc=ae)
                        gelu_done[ps] = ae.n
                kb.barrier("l1ffn_fc")
                sxa.close()
                if DBG_STOP == "l1f_fc":
                    return
                with ExitStack() as px:
                    emit_ln_to_bf(kb, st, px, D4C, W,
                                  dst_bf=lambda kc: hbf[:, kc * W:(kc + 1) * W],
                                  eps_col=C_EPS6, inv=1.0 / D4,
                                  bf_src=lambda kc: hbf[:, kc * W:(kc + 1) * W])
                kb.barrier("l1ffn_hln")
                if DBG_STOP == "l1f_hln":
                    return
                with ExitStack() as px:
                    emit_absmax_clamp(kb, st, px,
                                      lambda kc: hbf[:, kc * W:(kc + 1) * W], D4C, c0, W)
                if DBG_STOP == "l1f_am2":
                    return
                with ExitStack() as pp:
                    tsp = TernStream(kb, pp, st, "c_tsp" + str(half), D, st.thrp, st.nthrp)
                    psum_p = [pp.enter_context(nc.psum_tensor(f"c_pp{half}{i}", [128, W], F32))
                              for i in range(2)]
                    evac_done = [0, 0]

                    def pr_slab(i):
                        return st.d["l1_wpr_t"].ap()[i // 4][:, (i % 4) * D:(i % 4 + 1) * D]

                    tsp.load(0, pr_slab(0))
                    for oc in range(DC):
                        ps = oc % 2
                        kb.wait("tensor", ve, evac_done[ps])
                        for qtr in range(4):
                            i = oc * 4 + qtr
                            if i + 1 < DC * 4:
                                tsp.load(i + 1, pr_slab(i + 1))
                            wsl = tsp.use(i)
                            for kc in range(DC):
                                kk = qtr * DC + kc
                                kb.op("tensor", lambda h, o=psum_p[ps][:, :],
                                             l=wsl[:, kc * 128:(kc + 1) * 128],
                                             r=hbf[:, kk * W:(kk + 1) * W],
                                             s0=(kk == 0), s1=(kk == D4C - 1):
                                          h.matmul(o, l, r, start=s0, stop=s1))
                            tsp.release(i)
                        t_pp = pe.n
                        kb.op("vector", lambda h, d=st.xs[:, oc * TOK + c0: oc * TOK + c0 + W],
                                     s=psum_p[ps][:, :], g=st.gpr[:, :]:
                              h.scalar_tensor_tensor(d, s, g, d, ALU.mult, ALU.add),
                              wait=[(pe, t_pp)], inc=ve)
                        evac_done[ps] = ve.n
                kb.barrier("l1ffn_proj")
                if DBG_STOP == "l1f_proj":
                    return
    emit_postln_stream(kb, st, eps_col=C_EPS5)


STAGE_EMITTERS = {
    "l0attn": emit_l0attn,
    "l0ffn": emit_l0ffn,
    "l1attn": emit_l1attn,
    "l1ffn": emit_l1ffn,
}

STAGE_INPUTS = {
    "l0attn": ["l0_wq_t", "l0_wk_t", "l0_wv", "l0_bq", "l0_bk", "l0_bv_row",
               "l0_wo_t", "l0_bo"],
    "l0ffn": ["l0_wg_t", "l0_wu_t", "l0_wd_t", "l0_gs", "l0_us", "l0_ds"],
    "l1attn": ["l1_wq_t", "l1_wk_t", "l1_wv", "l1_wo_t", "l1_gamma", "l1_beta",
               "l1_mask"],
    "l1ffn": ["l1_wfc_t", "l1_wpr_t"],
}


HPERM = []
_p0 = [h for h in range(NQ) if (h // 4) % 2 == 0]
_p1 = [h for h in range(NQ) if (h // 4) % 2 == 1]
for _c in range(16):
    HPERM += [_p0[_c], _p1[_c]]


def _host_arrays(inputs):
    a = {}
    wq = np.asarray(inputs["L0_attn_Wqkv"][:, :2048]).reshape(D, NQ, HD)
    a["l0_wq_t"] = _formB(np.ascontiguousarray(wq[:, HPERM, :].reshape(D, 2048)))
    a["l0_wk_t"] = _formB(inputs["L0_attn_Wqkv"][:, 2048:2560])
    a["l0_wv"] = np.ascontiguousarray(inputs["L0_attn_Wqkv"][:, 2560:])
    bq = np.asarray(inputs["L0_attn_bqkv"][:2048]).reshape(NQ, HD)
    a["l0_bq"] = _cols128(np.ascontiguousarray(bq[HPERM].reshape(2048)))
    a["l0_bk"] = _cols128(inputs["L0_attn_bqkv"][2048:2560])
    a["l0_bv_row"] = np.ascontiguousarray(inputs["L0_attn_bqkv"][2560:]).reshape(1, 512)
    wo = np.asarray(inputs["L0_attn_Wo"]).reshape(NQ, HD, D)
    a["l0_wo_t"] = _formB(np.ascontiguousarray(wo[HPERM].reshape(D, D)))
    a["l0_bo"] = _cols128(inputs["L0_attn_bo"])
    a["l0_wg_t"] = _formB(np.ascontiguousarray(inputs["L0_ffn_gate_w"].T))
    a["l0_wu_t"] = _formB(np.ascontiguousarray(inputs["L0_ffn_up_w"].T))
    a["l0_wd_t"] = _formB(np.ascontiguousarray(inputs["L0_ffn_down_w"].T))
    a["l0_gs"] = _cols128(inputs["L0_ffn_gate_s"])
    a["l0_us"] = _cols128(inputs["L0_ffn_up_s"])
    a["l0_ds"] = _cols128(inputs["L0_ffn_down_s"])
    a["l1_wq_t"] = _formB(inputs["L1_attn_qkv"][:, :2048])
    a["l1_wk_t"] = _formB(inputs["L1_attn_qkv"][:, 2048:4096])
    a["l1_wv"] = np.ascontiguousarray(inputs["L1_attn_qkv"][:, 4096:])
    a["l1_wo_t"] = _formB(inputs["L1_attn_o_proj"])
    a["l1_gamma"] = _cols128(inputs["L1_attn_gamma"])
    a["l1_beta"] = _cols128(inputs["L1_attn_beta"])
    a["l1_wfc_t"] = _formB(np.ascontiguousarray(inputs["L1_ffn_w_fc"].T))
    a["l1_wpr_t"] = _formB(np.ascontiguousarray(inputs["L1_ffn_w_proj"].T))
    return a


def _const_arrays():
    c = {}
    consts = np.zeros((128, 8), np.float32)
    consts[:, C_EPS5] = 1e-5
    consts[:, C_EPS6] = 1e-6
    consts[:, C_128] = 128.0
    consts[:, C_N128] = -128.0
    consts[:, C_HEPS] = 0.5e-6
    consts[:, C_NHEPS] = -0.5e-6
    c["consts"] = consts
    c["ones_bf"] = np.ones((128, 128), ml_dtypes.bfloat16)
    c["ones_f32"] = np.ones((128, 128), np.float32)
    c["id_bf"] = np.eye(128, dtype=ml_dtypes.bfloat16)
    c["id_f32"] = np.eye(128, dtype=np.float32)
    return c


def _l1_masks():
    out = []
    kidx = np.arange(SEQ)[None, :]
    for parity in range(2):
        m = np.zeros((8, 128, SEQ), np.float32)
        for j in range(8):
            q = parity * 1024 + j * 128 + np.arange(128)[:, None]
            m[j] = np.where(kidx <= q, 0.0, NEG)
        out.append(m.astype(ml_dtypes.bfloat16))
    return out


_DT = {np.dtype(np.float32): F32, np.dtype(ml_dtypes.bfloat16): BF16}


def _build_program(stages, host_shapes):
    nc = bass.Bass("TRN2", target_bir_lowering=False, debug=False,
                   detect_race_conditions=RACE_DETECT)
    ctx = ExitStack()
    st = St()
    decls = [("x_in", (D, TOK), F32), ("consts", (128, 8), F32),
             ("ones_bf", (128, 128), BF16), ("ones_f32", (128, 128), F32),
             ("id_bf", (128, 128), BF16), ("id_f32", (128, 128), F32)]
    for s in stages:
        for name in STAGE_INPUTS[s]:
            shp, dt = host_shapes[name]
            decls.append((name, shp, dt))
    alloc_persistent(nc, ctx, st, decls)
    st.d["x_out"] = nc.dram_tensor("x_out", [D, TOK], F32, kind="ExternalOutput")
    kb = KB(nc, ctx)
    emit_prologue(kb, st)
    for s in stages:
        STAGE_EMITTERS[s](kb, st)
    emit_epilogue(kb, st)
    with nc.Block() as block:
        kb.finalize(block)
    ctx.close()
    return nc


def _run(inputs, launch_groups):
    host = _host_arrays(inputs)
    consts = _const_arrays()
    masks = _l1_masks()
    x = np.asarray(inputs["L0_attn_x"])
    xcore = []
    for c in range(NCORES):
        seq, half = c // 2, c % 2
        xcore.append(np.ascontiguousarray(x[seq, half * TOK:(half + 1) * TOK, :].T))
    host_shapes = {}
    for k, v in host.items():
        host_shapes[k] = (v.shape, _DT[v.dtype])
    host_shapes["l1_mask"] = ((8, 128, SEQ), BF16)
    for grp in launch_groups:
        nc = _build_program(grp, host_shapes)
        in_maps = []
        for c in range(NCORES):
            m = {"x_in": xcore[c]}
            m.update(consts)
            for s in grp:
                for name in STAGE_INPUTS[s]:
                    m[name] = masks[c % 2] if name == "l1_mask" else host[name]
            in_maps.append(m)
        res = run_bass_kernel_spmd(nc, in_maps, core_ids=list(range(NCORES)))
        xcore = [np.ascontiguousarray(res.results[c]["x_out"]) for c in range(NCORES)]
    out = np.empty((4, SEQ, D), np.float32)
    for c in range(NCORES):
        seq, half = c // 2, c % 2
        out[seq, half * TOK:(half + 1) * TOK, :] = xcore[c].T
    return out


def kernel(**inputs):
    return _run(inputs, LAUNCH_GROUPS)


# version 07
# speedup vs baseline: 1.7991x; 1.7991x over previous
import math
from contextlib import ExitStack

import numpy as np
import ml_dtypes

import concourse.bass as bass
import concourse.mybir as mybir
from concourse.bass_utils import run_bass_kernel_spmd

F32 = mybir.dt.float32
BF16 = mybir.dt.bfloat16
AF = mybir.ActivationFunctionType
ALU = mybir.AluOpType
AX = mybir.AxisListType

NCORES = 8
TOK = 1024
SEQ = 2048
D = 2048
DC = D // 128
INTER = 5632
IC = INTER // 128
D4 = 8192
D4C = D4 // 128
NQ, NKV, HD = 32, 8, 64
KVF = NKV * HD
NEG = -1.0e9

C_EPS5, C_EPS6, C_128, C_N128, C_HEPS, C_NHEPS = 0, 1, 2, 3, 4, 5

ENGINES = ("sync", "tensor", "vector", "scalar", "gpsimd")

ALL_STAGES = ("l0attn", "l0ffn", "l1attn", "l1ffn")
DBG_STOP = None
AM_STOP = None
RACE_DETECT = False
TS_SAFE = True
LAUNCH_GROUPS = [ALL_STAGES]



def _formB(w):
    K, M = w.shape
    return np.ascontiguousarray(
        w.reshape(K // 128, 128, M // 128, 128).transpose(2, 1, 0, 3).reshape(M // 128, 128, K))


def _cols128(v):
    return np.ascontiguousarray(np.asarray(v).reshape(-1, 128).T)



class Sem:
    __slots__ = ("h", "n")

    def __init__(self, ctx, nc, name):
        self.h = ctx.enter_context(nc.semaphore(name))
        self.n = 0


class KB:
    def __init__(self, nc, ctx):
        self.nc = nc
        self.ctx = ctx
        self.q = {e: [] for e in ENGINES}
        self.tags = {e: [] for e in ENGINES}
        self.cur_tag = "init"
        self._sems = {}

    def sem(self, name):
        if name not in self._sems:
            self._sems[name] = Sem(self.ctx, self.nc, name)
        return self._sems[name]

    def raw(self, eng, fn):
        self.q[eng].append(fn)
        self.tags[eng].append(self.cur_tag)

    def wait(self, eng, sem, val):
        if val > 0:
            self.raw(eng, lambda h, s=sem.h, v=val: h.wait_ge(s, v))

    def op(self, eng, fn, wait=(), inc=None, amt=1):
        for (s, v) in wait:
            self.wait(eng, s, v)
        if inc is not None:
            inc.n += amt
            self.raw(eng, lambda h, fn=fn, s=inc.h, k=amt: fn(h).then_inc(s, k))
        else:
            self.raw(eng, lambda h, fn=fn: fn(h))

    def mark(self, eng, sem, amt=1):
        sem.n += amt
        old = self.q[eng][-1]
        def wrapped(h, old=old, sh=sem.h, k=amt):
            inst = old(h)
            return inst.then_inc(sh, k)
        self.q[eng][-1] = wrapped

    def dma(self, eng, out, in_, sem=None):
        if sem is None:
            sem = self.sem("ld")
        self.op(eng, lambda h, o=out, i=in_: h.dma_start(out=o, in_=i), inc=sem, amt=16)
        return sem.n

    def barrier(self, tag=""):
        s = self.sem("bar")
        s.n += len(ENGINES)
        target = s.n
        for e in ENGINES:
            self.raw(e, lambda h, sh=s.h: h.sem_inc(sh, 1))
        for e in ENGINES:
            self.wait(e, s, target)

    def finalize(self, block):
        nc = self.nc
        def make(ops, tags):
            def run(handle):
                i = 0
                n = len(ops)
                while i < n:
                    tag = tags[i]
                    j = i
                    with nc.named_scope(tag):
                        while j < n and tags[j] == tag:
                            ops[j](handle)
                            j += 1
                    i = j
            return run
        for e in ENGINES:
            getattr(block, e)(make(self.q[e], self.tags[e]))


class WStream:

    def __init__(self, kb, sx, name, width):
        nc = kb.nc
        self.kb = kb
        self.width = width
        self.stage = sx.enter_context(nc.sbuf_tensor(f"{name}_s", [128, width], F32))
        self.bf = [sx.enter_context(nc.sbuf_tensor(f"{name}_b{i}", [128, width], BF16))
                   for i in range(2)]
        self.release_cnt = [0, 0]
        self.cast_cnt = 0
        self.ready_cnt = [0, 0]

    def load(self, i, dram_ap, width=None, nch=None):
        kb = self.kb
        w = width or self.width
        s = i % 2
        ld, ae, pe = kb.sem("ld"), kb.sem("ae"), kb.sem("pe")
        kb.wait("sync", ae, self.cast_cnt)
        dst = self.stage[:, 0:w]
        if nch is not None:
            dst = dst.rearrange("p (c n) -> p c n", c=nch)
        t = kb.dma("sync", dst, dram_ap)
        kb.op("scalar", lambda h, d=self.bf[s][:, 0:w], x=self.stage[:, 0:w]:
              h.copy(d, x), wait=[(ld, t), (pe, self.release_cnt[s])], inc=ae)
        self.cast_cnt = ae.n
        self.ready_cnt[s] = ae.n

    def use(self, i):
        s = i % 2
        self.kb.wait("tensor", self.kb.sem("ae"), self.ready_cnt[s])
        return self.bf[s]

    def release(self, i):
        pe = self.kb.sem("pe")
        self.kb.mark("tensor", pe)
        self.release_cnt[i % 2] = pe.n


class TernStream:

    def __init__(self, kb, sx, st, name, width, thr, nthr):
        nc = kb.nc
        self.kb = kb
        self.st = st
        self.width = width
        self.thr, self.nthr = thr, nthr
        self.stage = sx.enter_context(nc.sbuf_tensor(f"{name}_s", [128, width], F32))
        self.tneg = [sx.enter_context(nc.sbuf_tensor(f"{name}_n{i}", [128, width], BF16))
                     for i in range(2)]
        self.tbf = [sx.enter_context(nc.sbuf_tensor(f"{name}_t{i}", [128, width], BF16))
                    for i in range(2)]
        self.ready = [0, 0]
        self.mm_done = [0, 0]
        self.neg_read = [0, 0]
        self.stage_read = (0, 0)

    def load(self, i, dram_ap):
        kb, st = self.kb, self.st
        s = i % 2
        ld, ve, pe, ge = kb.sem("ld"), kb.sem("ve"), kb.sem("pe"), kb.sem("ge")
        if TS_SAFE:
            kb.wait("sync", ve, ve.n)
        t_w = kb.dma("sync", self.stage[:, :], dram_ap)
        kb.op("vector", lambda h, d=self.tneg[s][:, :], w=self.stage[:, :], t=self.nthr[:, :]:
              h.tensor_scalar(d, w, t, None, ALU.is_le),
              wait=[(ld, t_w), (pe, self.mm_done[s])], inc=ve)
        kb.op("vector", lambda h, d=self.tbf[s][:, :], w=self.stage[:, :], t=self.thr[:, :],
                     n=self.tneg[s][:, :]:
              h.scalar_tensor_tensor(d, w, t, n, ALU.is_ge, ALU.subtract),
              inc=ve)
        self.ready[s] = ve.n
        self.neg_read[s] = ve.n
        self.stage_read = (0, ve.n)

    def use(self, i):
        s = i % 2
        self.kb.wait("tensor", self.kb.sem("ve"), self.ready[s])
        return self.tbf[s]

    def release(self, i):
        pe = self.kb.sem("pe")
        self.kb.mark("tensor", pe)
        self.mm_done[i % 2] = pe.n



class St:
    pass


def alloc_persistent(nc, ctx, st, input_decls):
    st.nc = nc
    st.xs = ctx.enter_context(nc.sbuf_tensor("xs", [128, DC * TOK], F32))
    st.xb = ctx.enter_context(nc.sbuf_tensor("xb", [128, DC * TOK], BF16))
    st.ones_bf = ctx.enter_context(nc.sbuf_tensor("sb_ones_bf", [128, 128], BF16))
    st.ones_f32 = ctx.enter_context(nc.sbuf_tensor("sb_ones_f32", [128, 128], F32))
    st.id_bf = ctx.enter_context(nc.sbuf_tensor("sb_id_bf", [128, 128], BF16))
    st.id_f32 = ctx.enter_context(nc.sbuf_tensor("sb_id_f32", [128, 128], F32))
    st.consts = ctx.enter_context(nc.sbuf_tensor("sb_consts", [128, 8], F32))
    st.stats = ctx.enter_context(nc.sbuf_tensor("stats", [128, 16], F32))
    st._ln_id = 0
    st._uid = [0]
    st.d = {}
    for name, shape, dt in input_decls:
        st.d[name] = nc.dram_tensor(name, list(shape), dt, kind="ExternalInput")


def emit_prologue(kb, st):
    ld, ae = kb.sem("ld"), kb.sem("ae")
    kb.dma("sync", st.xs[:, :].rearrange("p (c t) -> p c t", c=DC),
           st.d["x_in"].ap().rearrange("(c p) t -> p c t", p=128))
    kb.dma("sync", st.consts[:, :], st.d["consts"].ap())
    kb.dma("sync", st.ones_bf[:, :], st.d["ones_bf"].ap())
    kb.dma("sync", st.ones_f32[:, :], st.d["ones_f32"].ap())
    kb.dma("sync", st.id_bf[:, :], st.d["id_bf"].ap())
    t_all = kb.dma("sync", st.id_f32[:, :], st.d["id_f32"].ap())
    for kc in range(DC):
        kb.op("scalar", lambda h, d=st.xb[:, kc * TOK:(kc + 1) * TOK],
                     s=st.xs[:, kc * TOK:(kc + 1) * TOK]: h.copy(d, s),
              wait=[(ld, t_all)] if kc == 0 else [], inc=ae)
    kb.barrier("prologue")


def emit_epilogue(kb, st):
    kb.barrier("prestore")
    so = kb.sem("stdone")
    kb.op("gpsimd", lambda h, o=st.d["x_out"].ap().rearrange("(c p) t -> p c t", p=128),
                 i=st.xs[:, :].rearrange("p (c t) -> p c t", c=DC):
          h.dma_start(out=o, in_=i), inc=so, amt=16)
    kb.wait("gpsimd", so, so.n)



def _ln_stats(kb, st, px, ndc, c0, W, inv, eps_col, bf_src=None, f32_src=None):
    nc = st.nc
    pe, ve, ae = kb.sem("pe"), kb.sem("ve"), kb.sem("ae")
    nbl = (W + 511) // 512
    st._ln_id += 1
    nm = f"ln{st._ln_id}"
    st.m2 = px.enter_context(nc.sbuf_tensor(nm + "_m2", [128, W], F32))
    st.var = px.enter_context(nc.sbuf_tensor(nm + "_var", [128, W], F32))
    st.rstd = px.enter_context(nc.sbuf_tensor(nm + "_rstd", [128, W], F32))
    st.lntmp = px.enter_context(nc.sbuf_tensor(nm + "_tmp", [128, W], F32))
    st.sq = [px.enter_context(nc.sbuf_tensor(nm + f"_sq{i}", [128, W], BF16)) for i in range(2)]
    st.stbf = [px.enter_context(nc.sbuf_tensor(nm + f"_sb{i}", [128, W], BF16)) for i in range(2)]
    psum_sum = px.enter_context(nc.psum_tensor(nm + "_psum", [128, W], F32))
    psum_ssq = px.enter_context(nc.psum_tensor(nm + "_pssq", [128, W], F32))
    for kc in range(ndc):
        pp = kc % 2
        war = [(pe, pe.n)] if kc >= 2 else []
        if bf_src is not None:
            src_bf = bf_src(kc)
            sq_src = src_bf
        else:
            kb.op("scalar", lambda h, d=st.stbf[pp][:, 0:W], s=f32_src(kc):
                  h.copy(d, s), wait=war, inc=ae)
            src_bf = st.stbf[pp][:, 0:W]
            sq_src = f32_src(kc)
            war = []
        kb.op("scalar", lambda h, d=st.sq[pp][:, 0:W], s=sq_src:
              h.square(d, s), wait=war, inc=ae)
        t_ae = ae.n
        for nb in range(nbl):
            w0, w1 = nb * 512, min(W, (nb + 1) * 512)
            kb.op("tensor", lambda h, o=psum_sum[:, w0:w1], l=st.ones_bf[:, :],
                         r=src_bf[:, w0:w1], s0=(kc == 0), s1=(kc == ndc - 1):
                      h.matmul(o, l, r, start=s0, stop=s1),
                  wait=[(ae, t_ae)] if nb == 0 else [])
            kb.op("tensor", lambda h, o=psum_ssq[:, w0:w1], l=st.ones_bf[:, :],
                         r=st.sq[pp][:, w0:w1], s0=(kc == 0), s1=(kc == ndc - 1):
                      h.matmul(o, l, r, start=s0, stop=s1))
        kb.mark("tensor", pe)
    t_pe = pe.n
    kb.op("scalar", lambda h, d=st.m2[:, 0:W], s=psum_sum[:, :]:
          h.activation(d, s, AF.Square, scale=inv), wait=[(pe, t_pe)], inc=ae)
    t_ae = ae.n
    kb.op("vector", lambda h, d=st.var[:, 0:W], s=psum_ssq[:, :], m=st.m2[:, 0:W]:
          h.scalar_tensor_tensor(d, s, inv, m, ALU.mult, ALU.subtract),
          wait=[(ae, t_ae)], inc=ve)
    t_ve = ve.n
    kb.op("scalar", lambda h, d=st.var[:, 0:W], s=st.var[:, 0:W],
                 b=st.consts[:, eps_col:eps_col + 1]:
          h.activation(d, s, AF.Sqrt, bias=b), wait=[(ve, t_ve)], inc=ae)
    t_ae = ae.n
    kb.op("vector", lambda h, d=st.rstd[:, 0:W], s=st.var[:, 0:W]:
          h.reciprocal(d, s), wait=[(ae, t_ae)], inc=ve)
    return psum_sum, ve.n


def emit_postln_stream(kb, st, eps_col=C_EPS5):
    inv = 1.0 / D
    ve, ae = kb.sem("ve"), kb.sem("ae")
    with ExitStack() as px:
        psum_sum, t_rstd = _ln_stats(kb, st, px, DC, 0, TOK, inv, eps_col,
                                     f32_src=lambda kc: st.xs[:, kc * TOK:(kc + 1) * TOK])
        for kc in range(DC):
            dstf = st.xs[:, kc * TOK:(kc + 1) * TOK]
            kb.op("vector", lambda h, d=dstf, s=psum_sum[:, :], x=dstf:
                  h.scalar_tensor_tensor(d, s, -inv, x, ALU.mult, ALU.add),
                  wait=[(ve, t_rstd)] if kc == 0 else [], inc=ve)
            kb.op("vector", lambda h, d=dstf, a=dstf, b=st.rstd[:, :]:
                  h.tensor_mul(d, a, b), inc=ve)
            t_n = kb.sem("ve").n
            kb.op("scalar", lambda h, d=st.xb[:, kc * TOK:(kc + 1) * TOK], s=dstf:
                  h.copy(d, s), wait=[(ve, t_n)], inc=ae)
        kb.barrier("postln")


def emit_ln_to_bf(kb, st, px, ndc, W, dst_bf, eps_col, inv, bf_src=None, f32_src=None,
                  gamma_col=None, beta_col=None):
    ve, ae = kb.sem("ve"), kb.sem("ae")
    psum_sum, t_rstd = _ln_stats(kb, st, px, ndc, 0, W, inv, eps_col,
                                 bf_src=bf_src, f32_src=f32_src)
    for kc in range(ndc):
        srcap = bf_src(kc) if bf_src is not None else f32_src(kc)
        tmp = st.lntmp[:, 0:W]
        kb.op("vector", lambda h, d=tmp, s=psum_sum[:, :], x=srcap:
              h.scalar_tensor_tensor(d, s, -inv, x, ALU.mult, ALU.add),
              wait=[(ve, t_rstd), (ae, ae.n)], inc=ve)
        if gamma_col is None:
            kb.op("vector", lambda h, d=dst_bf(kc), a=tmp, b=st.rstd[:, 0:W]:
                  h.tensor_mul(d, a, b), inc=ve)
        else:
            kb.op("vector", lambda h, d=tmp, a=tmp, b=st.rstd[:, 0:W]:
                  h.tensor_mul(d, a, b), inc=ve)
            t_n = ve.n
            kb.op("scalar", lambda h, d=dst_bf(kc), s=tmp, g=gamma_col(kc), b=beta_col(kc):
                  h.activation(d, s, AF.Identity, bias=b, scale=g),
                  wait=[(ve, t_n)], inc=ae)
    kb.wait("vector", ae, kb.sem("ae").n)



def emit_l0attn(kb, st):
    nc = st.nc
    ld, pe, ve, ae, gd, cc = (kb.sem(x) for x in ("ld", "pe", "ve", "ae", "gd", "cc"))
    kvb = nc.dram_tensor("l0_kvb", [1024, 1024], BF16)
    kvg = nc.dram_tensor("l0_kvg", [2048, 1024], BF16)
    afd = nc.dram_tensor("l0_afd", [D, TOK], BF16)
    with ExitStack() as sx:
        with ExitStack() as p1:
            ws = WStream(kb, p1, "a_w", D)
            wvb = p1.enter_context(nc.sbuf_tensor("a_wv", [128, DC * 512], BF16))
            kown = p1.enter_context(nc.sbuf_tensor("a_kown", [128, 4 * TOK], BF16))
            vown = p1.enter_context(nc.sbuf_tensor("a_vown", [128, 8 * 512], BF16))
            bv = p1.enter_context(nc.sbuf_tensor("a_bv", [128, 512], F32))
            bvrow = p1.enter_context(nc.sbuf_tensor("a_bvrow", [1, 512], F32))
            bk = p1.enter_context(nc.sbuf_tensor("a_bk", [128, 4], F32))
            with ExitStack() as pp:
                psum_k = pp.enter_context(nc.psum_tensor("a_pk", [128, TOK], F32))
                psum_v = pp.enter_context(nc.psum_tensor("a_pv", [128, 512], F32))
                psum_bv = pp.enter_context(nc.psum_tensor("a_pbv", [128, 512], F32))
                wv_gate = 0
                for q in range(4):
                    kb.wait("sync", ae, wv_gate)
                    t_w = kb.dma("sync",
                                 ws.stage[:, :].rearrange("p (c n) -> p c n", c=4),
                                 st.d["l0_wv"].ap().rearrange("(c p) n -> p c n", p=128)
                                 [:, q * 4:(q + 1) * 4, :])
                    kb.op("scalar", lambda h, d=wvb[:, q * 2048:(q + 1) * 2048],
                                 s=ws.stage[:, :]: h.copy(d, s),
                          wait=[(ld, t_w)], inc=ae)
                    wv_gate = ae.n
                ws.cast_cnt = wv_gate
                t_wv = ae.n
                if DBG_STOP == "l0attn_wv":
                    return
                kb.dma("sync", bvrow[:, :], st.d["l0_bv_row"].ap())
                t_bk = kb.dma("sync", bk[:, :], st.d["l0_bk"].ap())
                t_bv = t_bk
                kb.op("tensor", lambda h, o=psum_bv[:, :], l=st.ones_f32[0:1, :], r=bvrow[:, :]:
                      h.matmul(o, l, r, start=True, stop=True), wait=[(ld, t_bv)], inc=pe)
                kb.op("vector", lambda h, d=bv[:, :], s=psum_bv[:, :]:
                      h.tensor_copy(d, s), wait=[(pe, pe.n)], inc=ve)
                t_bvb = ve.n
                if DBG_STOP == "l0attn_bv":
                    return
                kb.wait("tensor", ae, t_wv)
                for tt in range(8):
                    for kc in range(DC):
                        kb.op("tensor", lambda h, o=psum_v[:, :],
                                     l=st.xb[:, kc * TOK + tt * 128: kc * TOK + (tt + 1) * 128],
                                     r=wvb[:, kc * 512:(kc + 1) * 512],
                                     s0=(kc == 0), s1=(kc == DC - 1):
                                  h.matmul(o, l, r, start=s0, stop=s1),
                              wait=[(ve, ve.n)] if tt > 0 else [])
                    kb.mark("tensor", pe)
                    t_pv = pe.n
                    kb.op("vector", lambda h, d=vown[:, tt * 512:(tt + 1) * 512],
                                 s=psum_v[:, :], b=bv[:, :]:
                          h.tensor_add(d, s, b), wait=[(pe, t_pv), (ve, t_bvb)], inc=ve)
                if DBG_STOP == "l0attn_v":
                    return
                for m in range(4):
                    ws.load(m, st.d["l0_wk_t"].ap()[m])
                    wsl = ws.use(m)
                    for nb in range(2):
                        for kc in range(DC):
                            kb.op("tensor", lambda h, o=psum_k[:, nb * 512:(nb + 1) * 512],
                                         l=wsl[:, kc * 128:(kc + 1) * 128],
                                         r=st.xb[:, kc * TOK + nb * 512: kc * TOK + (nb + 1) * 512],
                                         s0=(kc == 0), s1=(kc == DC - 1):
                                      h.matmul(o, l, r, start=s0, stop=s1),
                                  wait=[(ae, ae.n)] if (nb == 0 and kc == 0 and m > 0) else [])
                    ws.release(m)
                    t_pk = pe.n
                    kb.op("scalar", lambda h, d=kown[:, m * TOK:(m + 1) * TOK], s=psum_k[:, :],
                                 b=bk[:, m:m + 1]:
                          h.activation(d, s, AF.Identity, bias=b),
                          wait=[(pe, t_pk), (ld, t_bk)], inc=ae)
            if DBG_STOP == "l0attn_k":
                return
            t_evac = (ae.n, ve.n)
            kb.wait("gpsimd", ae, t_evac[0])
            kb.wait("gpsimd", ve, t_evac[1])
            kb.dma("gpsimd", kvb.ap()[0:512, :].rearrange("(c p) t -> p c t", p=128),
                   kown[:, :].rearrange("p (c t) -> p c t", c=4), sem=gd)
            t_gd = kb.dma("gpsimd",
                          kvb.ap()[512:1024, :].rearrange("r (t h) -> (r t) h", t=2)
                          .rearrange("(tt p) h -> p tt h", p=128),
                          vown[:, :].rearrange("p (tt h) -> p tt h", tt=8), sem=gd)
            kb.wait("gpsimd", gd, t_gd)
            if DBG_STOP == "l0attn_dmaout":
                return
            kb.op("gpsimd", lambda h: h.collective_compute(
                "AllGather", ALU.bypass,
                replica_groups=[[0, 1], [2, 3], [4, 5], [6, 7]],
                ins=[kvb.ap().opt()], outs=[kvg.ap().opt()]), inc=cc)
        if DBG_STOP == "l0attn_p1":
            return
        with ExitStack() as p2:
            kfull = p2.enter_context(nc.sbuf_tensor("a_kfull", [128, 4 * SEQ], BF16))
            vfull = p2.enter_context(nc.sbuf_tensor("a_vfull", [128, DC * 512], BF16))
            atn = p2.enter_context(nc.sbuf_tensor("a_atn", [128, DC * 512], BF16))
            sexp = [p2.enter_context(nc.sbuf_tensor(f"a_sexp{i}", [128, SEQ], BF16))
                    for i in range(2)]
            qg = p2.enter_context(nc.sbuf_tensor("a_qg", [128, DC * TOK], BF16))
            afb = [p2.enter_context(nc.sbuf_tensor(f"a_afb{i}", [128, 512], BF16)) for i in range(2)]
            afb_done = [0, 0]
            nmax = [p2.enter_context(nc.sbuf_tensor(f"a_nmax{i}", [128, 1], F32)) for i in range(2)]
            lsum = [p2.enter_context(nc.sbuf_tensor(f"a_lsum{i}", [128, 1], F32)) for i in range(2)]
            rl = [p2.enter_context(nc.sbuf_tensor(f"a_rl{i}", [128, 1], F32)) for i in range(2)]
            bqq = p2.enter_context(nc.sbuf_tensor("a_bqq", [128, 16], F32))
            ws = WStream(kb, p2, "a_w2", D)
            kb.wait("sync", cc, cc.n)
            for r in range(2):
                kb.dma("sync", kfull[:, :].rearrange("p (c t) -> p c t", c=4)[:, :, r * TOK:(r + 1) * TOK],
                       kvg.ap()[r * 1024: r * 1024 + 512, :].rearrange("(c p) t -> p c t", p=128))
                kb.dma("sync",
                       vfull[:, :].rearrange("p (c v) -> p c v", c=DC)[:, r * 8:(r + 1) * 8, :],
                       kvg.ap()[r * 1024 + 512: (r + 1) * 1024, :]
                       .rearrange("r2 (t h) -> (r2 t) h", t=2)
                       .rearrange("(tt p) h -> p tt h", p=128))
            t_bq = kb.dma("sync", bqq[:, :], st.d["l0_bq"].ap())
            kb.op("vector", lambda h, d=bqq[:, :], s=bqq[:, :]:
                  h.tensor_scalar_mul(d, s, 0.125), wait=[(ld, t_bq)], inc=ve)
            t_bqq = ve.n
            t_kv = kb.sem("ld").n
            with ExitStack() as pq:
                psum_q = pq.enter_context(nc.psum_tensor("a_pq", [128, TOK], F32))
                kb.wait("tensor", ld, t_kv)
                for m in range(DC):
                    ws.load(m, st.d["l0_wq_t"].ap()[m])
                    wsl = ws.use(m)
                    kb.wait("tensor", ae, ae.n)
                    for nb in range(2):
                        for kc in range(DC):
                            kb.op("tensor", lambda h, o=psum_q[:, nb * 512:(nb + 1) * 512],
                                         l=wsl[:, kc * 128:(kc + 1) * 128],
                                         r=st.xb[:, kc * TOK + nb * 512: kc * TOK + (nb + 1) * 512],
                                         s0=(kc == 0), s1=(kc == DC - 1):
                                      h.matmul(o, l, r, start=s0, stop=s1))
                    ws.release(m)
                    t_pq = pe.n
                    kb.op("scalar", lambda h, d=qg[:, m * TOK:(m + 1) * TOK], s=psum_q[:, :],
                                 b=bqq[:, m:m + 1]:
                          h.activation(d, s, AF.Identity, bias=b, scale=0.125),
                          wait=[(pe, t_pq)] + ([(ve, t_bqq)] if m == 0 else []),
                          inc=ae)
            t_qdone = kb.sem("ae").n
            with ExitStack() as pp:
                PS = [pp.enter_context(nc.psum_tensor(f"a_PS{p}", [128, SEQ], F32))
                      for p in range(2)]
                PSB = [PS[p][:, :].bitcast(BF16) for p in range(2)]
                AV = PS[1][:, 1024:1536]
                kb.wait("tensor", ae, t_qdone)

                NJ = NQ * 4
                sc_pe = {}
                tr_done = [0, 0]
                st_done = [t_qdone, t_qdone]
                norm_ve = {}
                exp_ae = [0, 0]
                av_evac = [0]
                av_pe = [0]

                def head_info(hi):
                    h_ = HPERM[hi]
                    jv = h_ // 4
                    return (64 * (hi % 2), hi // 2, 64 * (jv % 2), jv // 2, jv)

                def emit_scores(g, jj):
                    p = jj % 2
                    hi, qs = jj // 4, jj % 4
                    qp, qc, kvp, kvc, jv = head_info(hi)
                    kb.wait("tensor", ae, st_done[p])
                    if p == 1:
                        kb.wait("tensor", ae, av_evac[0])
                    for nb in range(4):
                        kb.op("tensor", lambda h, o=PS[p][:, nb * 512:(nb + 1) * 512],
                                     l=qg[qp:qp + 64, qc * TOK + g * 512 + qs * 128:
                                          qc * TOK + g * 512 + (qs + 1) * 128],
                                     r=kfull[kvp:kvp + 64, kvc * SEQ + nb * 512: kvc * SEQ + (nb + 1) * 512]:
                                  h.matmul(o, l, r, start=True, stop=True))
                    kb.mark("tensor", pe)
                    sc_pe[jj] = pe.n

                def emit_chain(jj):
                    p = jj % 2
                    kb.op("vector", lambda h, d=nmax[p][:, :], s=PS[p][:, :]:
                          h.reduce_max(d, s, axis=AX.X, negate=True),
                          wait=[(pe, sc_pe[jj]), (ae, exp_ae[p])], inc=ve)
                    t_mx = ve.n
                    kb.op("scalar", lambda h, d=sexp[p][:, :], s=PS[p][:, :],
                                 b=nmax[p][:, :], a=lsum[p][:, :]:
                          h.activation(d, s, AF.Exp, bias=b, accum_out=a),
                          wait=[(ve, t_mx), (pe, tr_done[p])], inc=ae)
                    exp_ae[p] = ae.n
                    kb.op("vector", lambda h, d=rl[p][:, :], s=lsum[p][:, :]:
                          h.reciprocal(d, s), wait=[(ae, ae.n)], inc=ve)
                    t_rl = ve.n
                    kb.op("vector", lambda h, d=sexp[p][:, :], s=sexp[p][:, :], r=rl[p][:, :]:
                          h.tensor_scalar_mul(d, s, r), wait=[(ve, t_rl)], inc=ve)
                    norm_ve[jj] = ve.n

                def emit_transposes(jj):
                    p = jj % 2
                    kb.wait("tensor", ve, norm_ve[jj])
                    for kc in range(DC):
                        kb.op("tensor", lambda h, o=PSB[p][:, kc * 128:(kc + 1) * 128],
                                     i=sexp[p][:, kc * 128:(kc + 1) * 128], d=st.id_bf[:, :]:
                                  h.transpose(o, i, d))
                    kb.mark("tensor", pe)
                    tr_done[p] = pe.n

                def emit_stripes(jj):
                    p = jj % 2
                    qs = jj % 4
                    w = [(pe, tr_done[p])]
                    if qs == 0:
                        w.append((pe, av_pe[0]))
                    kb.op("scalar", lambda h,
                                 d=atn[:, :].rearrange("p (c n) -> p c n", c=DC)[:, :, qs * 128:(qs + 1) * 128],
                                 s=PSB[p][:, 0:2048].rearrange("p (c n) -> p c n", c=DC):
                          h.copy(d, s), wait=w, inc=ae)
                    st_done[p] = ae.n

                def emit_attnv(g, hi):
                    _, _, _, _, jv = head_info(hi)
                    kb.wait("tensor", ae, max(st_done))
                    for kc in range(DC):
                        kb.op("tensor", lambda h, o=AV[0:64, :],
                                     l=vfull[:, kc * 512 + jv * 64: kc * 512 + (jv + 1) * 64],
                                     r=atn[:, kc * 512:(kc + 1) * 512],
                                     s0=(kc == 0), s1=(kc == DC - 1):
                                  h.matmul(o, l, r, start=s0, stop=s1))
                    kb.mark("tensor", pe)
                    av_pe[0] = pe.n
                    t_av = pe.n
                    s_ = hi % 2
                    qp = 64 * (hi % 2)
                    kb.op("scalar", lambda h, d=afb[s_][0:64, :], s=AV[0:64, :]:
                          h.copy(d, s), wait=[(pe, t_av), (kb.sem("gd"), afb_done[s_])],
                          inc=ae)
                    av_evac[0] = ae.n
                    t_afb = ae.n
                    kb.wait("gpsimd", ae, t_afb)
                    kb.dma("gpsimd",
                           afd.ap()[hi * 64:(hi + 1) * 64, g * 512:(g + 1) * 512],
                           afb[s_][0:64, :], sem=kb.sem("gd"))
                    afb_done[s_] = kb.sem("gd").n

                for g in range(2):
                    emit_scores(g, 0)
                    emit_chain(0)
                    for jj in range(1, NJ):
                        emit_scores(g, jj)
                        emit_chain(jj)
                        emit_transposes(jj - 1)
                        emit_stripes(jj - 1)
                        if (jj - 1) % 4 == 3:
                            emit_attnv(g, (jj - 1) // 4)
                    emit_transposes(NJ - 1)
                    emit_stripes(NJ - 1)
                    emit_attnv(g, NQ - 1)
                    sc_pe.clear(); norm_ve.clear()
            kb.barrier("l0attn_p2")
        if DBG_STOP == "l0attn_p2":
            return
        with ExitStack() as p3:
            ws = WStream(kb, p3, "a_w3", D)
            af = p3.enter_context(nc.sbuf_tensor("a_af", [128, DC * TOK], BF16))
            bo = p3.enter_context(nc.sbuf_tensor("a_bo", [128, 16], F32))
            t_bo = kb.dma("sync", bo[:, :], st.d["l0_bo"].ap())
            kb.wait("sync", kb.sem("gd"), kb.sem("gd").n)
            t_af = kb.dma("sync", af[:, :].rearrange("p (c t) -> p c t", c=DC),
                          afd.ap().rearrange("(c p) t -> p c t", p=128))
            kb.wait("tensor", ld, t_af)
            with ExitStack() as pp:
                psum_op = pp.enter_context(nc.psum_tensor("a_pop", [128, TOK], F32))
                for m in range(DC):
                    ws.load(m, st.d["l0_wo_t"].ap()[m])
                    wsl = ws.use(m)
                    kb.wait("tensor", ve, ve.n)
                    for nb in range(2):
                        for kc in range(DC):
                            kb.op("tensor", lambda h, o=psum_op[:, nb * 512:(nb + 1) * 512],
                                         l=wsl[:, kc * 128:(kc + 1) * 128],
                                         r=af[:, kc * TOK + nb * 512: kc * TOK + (nb + 1) * 512],
                                         s0=(kc == 0), s1=(kc == DC - 1):
                                      h.matmul(o, l, r, start=s0, stop=s1))
                    ws.release(m)
                    t_pop = pe.n
                    kb.op("vector", lambda h, d=st.xs[:, m * TOK:(m + 1) * TOK],
                                 s=psum_op[:, :], b=bo[:, m:m + 1]:
                          h.scalar_tensor_tensor(d, s, b, d, ALU.add, ALU.add),
                          wait=[(pe, t_pop)] + ([(ld, t_bo)] if m == 0 else []), inc=ve)
            kb.barrier("l0attn_p3")
    emit_postln_stream(kb, st, eps_col=C_EPS5)



def emit_l0ffn(kb, st):
    nc = st.nc
    ld, pe, ve, ae = (kb.sem(x) for x in ("ld", "pe", "ve", "ae"))
    with ExitStack() as sx:
        hbf = sx.enter_context(nc.sbuf_tensor("f_h", [128, IC * TOK], BF16))
        with ExitStack() as p1:
            ws = WStream(kb, p1, "f_w", D)
            gs = p1.enter_context(nc.sbuf_tensor("f_gs", [128, IC], F32))
            us = p1.enter_context(nc.sbuf_tensor("f_us", [128, IC], F32))
            sg = p1.enter_context(nc.sbuf_tensor("f_sg", [128, TOK], F32))
            t_s = kb.dma("sync", gs[:, :], st.d["l0_gs"].ap())
            t_s = kb.dma("sync", us[:, :], st.d["l0_us"].ap())
            with ExitStack() as pp:
                psum_g = pp.enter_context(nc.psum_tensor("f_pg", [128, TOK], F32))
                psum_u = pp.enter_context(nc.psum_tensor("f_pu", [128, TOK], F32))
                kb.wait("scalar", ld, t_s)
                for m in range(IC):
                    ws.load(2 * m, st.d["l0_wg_t"].ap()[m])
                    ws.load(2 * m + 1, st.d["l0_wu_t"].ap()[m])
                    wg = ws.use(2 * m)
                    kb.wait("tensor", ae, ae.n)
                    for nb in range(2):
                        for kc in range(DC):
                            kb.op("tensor", lambda h, o=psum_g[:, nb * 512:(nb + 1) * 512],
                                         l=wg[:, kc * 128:(kc + 1) * 128],
                                         r=st.xb[:, kc * TOK + nb * 512: kc * TOK + (nb + 1) * 512],
                                         s0=(kc == 0), s1=(kc == DC - 1):
                                      h.matmul(o, l, r, start=s0, stop=s1))
                    ws.release(2 * m)
                    t_pg = pe.n
                    wu = ws.use(2 * m + 1)
                    kb.wait("tensor", ve, ve.n)
                    for nb in range(2):
                        for kc in range(DC):
                            kb.op("tensor", lambda h, o=psum_u[:, nb * 512:(nb + 1) * 512],
                                         l=wu[:, kc * 128:(kc + 1) * 128],
                                         r=st.xb[:, kc * TOK + nb * 512: kc * TOK + (nb + 1) * 512],
                                         s0=(kc == 0), s1=(kc == DC - 1):
                                      h.matmul(o, l, r, start=s0, stop=s1))
                    ws.release(2 * m + 1)
                    t_pu = pe.n
                    kb.op("scalar", lambda h, d=sg[:, :], s=psum_g[:, :], g=gs[:, m:m + 1]:
                          h.activation(d, s, AF.Silu, scale=g),
                          wait=[(pe, t_pg), (ve, ve.n)], inc=ae)
                    t_sg = ae.n
                    kb.op("vector", lambda h, d=hbf[:, m * TOK:(m + 1) * TOK],
                                 s=psum_u[:, :], u=us[:, m:m + 1], a=sg[:, :]:
                          h.scalar_tensor_tensor(d, s, u, a, ALU.mult, ALU.mult),
                          wait=[(pe, t_pu), (ae, t_sg)], inc=ve)
            kb.barrier("l0ffn_p1")
        with ExitStack() as p2:
            ds = p2.enter_context(nc.sbuf_tensor("f_ds", [128, 16], F32))
            wstage = p2.enter_context(nc.sbuf_tensor("f_dstage", [128, INTER // 2], F32))
            t_ds = kb.dma("sync", ds[:, :], st.d["l0_ds"].ap())
            wdb = [st.xb[:, 0:INTER], st.xb[:, 8192:8192 + INTER]]
            wdb_done = [0, 0]
            cast_gate = 0
            with ExitStack() as pp:
                psum_d = [pp.enter_context(nc.psum_tensor(f"f_pd{i}", [128, TOK], F32))
                          for i in range(2)]
                evac_done = [0, 0]

                def load_wd(m):
                    nonlocal cast_gate
                    s = m % 2
                    for hf in range(2):
                        kb.wait("sync", ae, cast_gate)
                        t_w = kb.dma("sync", wstage[:, :],
                                     st.d["l0_wd_t"].ap()[m][:, hf * (INTER // 2):(hf + 1) * (INTER // 2)])
                        kb.op("scalar", lambda h, d=wdb[s][:, hf * (INTER // 2):(hf + 1) * (INTER // 2)],
                                     w=wstage[:, :]: h.copy(d, w),
                              wait=[(ld, t_w), (pe, wdb_done[s])], inc=ae)
                        cast_gate = ae.n
                    return ae.n

                ready = [0, 0]
                ready[0] = load_wd(0)
                for m in range(DC):
                    s = m % 2
                    if m + 1 < DC:
                        ready[(m + 1) % 2] = load_wd(m + 1)
                    kb.wait("tensor", ae, ready[s])
                    ps = m % 2
                    kb.wait("tensor", ve, evac_done[ps])
                    for nb in range(2):
                        for kc in range(IC):
                            kb.op("tensor", lambda h, o=psum_d[ps][:, nb * 512:(nb + 1) * 512],
                                         l=wdb[s][:, kc * 128:(kc + 1) * 128],
                                         r=hbf[:, kc * TOK + nb * 512: kc * TOK + (nb + 1) * 512],
                                         s0=(kc == 0), s1=(kc == IC - 1):
                                      h.matmul(o, l, r, start=s0, stop=s1))
                    kb.mark("tensor", pe)
                    wdb_done[s] = pe.n
                    t_pd = pe.n
                    kb.op("vector", lambda h, d=st.xs[:, m * TOK:(m + 1) * TOK],
                                 s=psum_d[ps][:, :], c=ds[:, m:m + 1]:
                          h.scalar_tensor_tensor(d, s, c, d, ALU.mult, ALU.add),
                          wait=[(pe, t_pd)] + ([(ld, t_ds)] if m == 0 else []), inc=ve)
                    evac_done[ps] = ve.n
            kb.barrier("l0ffn_p2")
    emit_postln_stream(kb, st, eps_col=C_EPS5)



def emit_l1attn(kb, st):
    nc = st.nc
    ld, pe, ve, ae, gd, cc = (kb.sem(x) for x in ("ld", "pe", "ve", "ae", "gd", "cc"))
    kvb = nc.dram_tensor("l1_kvb", [6144, 1024], BF16)
    kvg = nc.dram_tensor("l1_kvg", [12288, 1024], BF16)
    scale = 1.0 / math.sqrt(2048.0)
    afd = nc.dram_tensor("l1_afd", [D, TOK], BF16)
    with ExitStack() as sx:
        xn = sx.enter_context(nc.sbuf_tensor("b_xn", [128, DC * TOK], BF16))
        gamma = sx.enter_context(nc.sbuf_tensor("b_gam", [128, 16], F32))
        beta = sx.enter_context(nc.sbuf_tensor("b_bet", [128, 16], F32))
        t_gb = kb.dma("sync", gamma[:, :], st.d["l1_gamma"].ap())
        t_gb = kb.dma("sync", beta[:, :], st.d["l1_beta"].ap())
        kb.wait("scalar", ld, t_gb)
        with ExitStack() as px:
            emit_ln_to_bf(kb, st, px, DC, TOK,
                          dst_bf=lambda kc: xn[:, kc * TOK:(kc + 1) * TOK],
                          eps_col=C_EPS5, inv=1.0 / D,
                          f32_src=lambda kc: st.xs[:, kc * TOK:(kc + 1) * TOK],
                          gamma_col=lambda kc: gamma[:, kc:kc + 1],
                          beta_col=lambda kc: beta[:, kc:kc + 1])
        kb.barrier("l1_ln")
        if DBG_STOP == "l1_ln":
            return
        with ExitStack() as p1:
            ws = WStream(kb, p1, "b_w", D)
            kown = [p1.enter_context(nc.sbuf_tensor(f"b_kown{i}", [128, TOK], BF16))
                    for i in range(2)]
            kown_done = [0, 0]
            with ExitStack() as pp:
                psum_k = pp.enter_context(nc.psum_tensor("b_pk", [128, TOK], F32))
                for m in range(DC):
                    ws.load(m, st.d["l1_wk_t"].ap()[m])
                    wsl = ws.use(m)
                    kb.wait("tensor", ae, ae.n)
                    for nb in range(2):
                        for kc in range(DC):
                            kb.op("tensor", lambda h, o=psum_k[:, nb * 512:(nb + 1) * 512],
                                         l=wsl[:, kc * 128:(kc + 1) * 128],
                                         r=xn[:, kc * TOK + nb * 512: kc * TOK + (nb + 1) * 512],
                                         s0=(kc == 0), s1=(kc == DC - 1):
                                      h.matmul(o, l, r, start=s0, stop=s1))
                    ws.release(m)
                    t_pk = pe.n
                    s = m % 2
                    kb.op("scalar", lambda h, d=kown[s][:, :], p=psum_k[:, :]:
                          h.copy(d, p), wait=[(pe, t_pk), (gd, kown_done[s])], inc=ae)
                    t_ko = ae.n
                    kb.wait("gpsimd", ae, t_ko)
                    kb.dma("gpsimd", kvb.ap()[m * 128:(m + 1) * 128, :], kown[s][:, :], sem=gd)
                    kown_done[s] = gd.n
            if DBG_STOP == "l1_k":
                return
            wsv = WStream(kb, p1, "b_wv", 8 * 512)
            with ExitStack() as pp:
                psum_v = [pp.enter_context(nc.psum_tensor(f"b_pv{t}", [128, 512], F32))
                          for t in range(8)]
                vtmp = [p1.enter_context(nc.sbuf_tensor(f"b_vt{i}", [128, 512], BF16))
                        for i in range(2)]
                vtmp_done = [0, 0]
                for nb in range(4):
                    for hf in range(2):
                        i = nb * 2 + hf
                        wsv.load(i, st.d["l1_wv"].ap()
                                 .rearrange("(c p) n -> p c n", p=128)
                                 [:, hf * 8:(hf + 1) * 8, nb * 512:(nb + 1) * 512],
                                 nch=8)
                        wsl = wsv.use(i)
                        if hf == 0:
                            kb.wait("tensor", ve, ve.n)
                            kb.wait("tensor", ae, ae.n)
                        for kc8 in range(8):
                            kc = hf * 8 + kc8
                            for tt in range(8):
                                kb.op("tensor", lambda h, o=psum_v[tt][:, :],
                                             l=xn[:, kc * TOK + tt * 128: kc * TOK + (tt + 1) * 128],
                                             r=wsl[:, kc8 * 512:(kc8 + 1) * 512],
                                             s0=(kc == 0), s1=(kc == DC - 1):
                                          h.matmul(o, l, r, start=s0, stop=s1))
                        wsv.release(i)
                    t_pv = pe.n
                    for tt in range(8):
                        s = tt % 2
                        kb.op("vector", lambda h, d=vtmp[s][:, :], s_=psum_v[tt][:, :]:
                              h.tensor_copy(d, s_),
                              wait=[(pe, t_pv), (gd, vtmp_done[s])], inc=ve)
                        t_vt = ve.n
                        kb.wait("gpsimd", ve, t_vt)
                        kb.dma("gpsimd",
                               kvb.ap()[2048 + tt * 256: 2048 + (tt + 1) * 256, :]
                               .rearrange("(p two) t -> p (two t)", two=2)
                               [:, nb * 512:(nb + 1) * 512],
                               vtmp[s][:, :], sem=gd)
                        vtmp_done[s] = gd.n
            if DBG_STOP == "l1_v":
                return
            kb.wait("gpsimd", gd, kb.sem("gd").n)
            kb.op("gpsimd", lambda h: h.collective_compute(
                "AllGather", ALU.bypass,
                replica_groups=[[0, 1], [2, 3], [4, 5], [6, 7]],
                ins=[kvb.ap().opt()], outs=[kvg.ap().opt()]), inc=cc)
        with ExitStack() as p2:
            qg = p2.enter_context(nc.sbuf_tensor("b_qg", [128, DC * TOK], BF16))
            atn = p2.enter_context(nc.sbuf_tensor("b_atn", [128, DC * 512], BF16))
            sexp = p2.enter_context(nc.sbuf_tensor("b_sexp", [128, SEQ], BF16))
            mask = [p2.enter_context(nc.sbuf_tensor("b_msk", [128, SEQ], BF16))]
            afb = [p2.enter_context(nc.sbuf_tensor(f"b_afb{i}", [128, 512], BF16)) for i in range(2)]
            afb_done = [0, 0]
            vtb_done = [0, 0, 0, 0]
            nmax = p2.enter_context(nc.sbuf_tensor("b_nmax", [128, 1], F32))
            lsum = p2.enter_context(nc.sbuf_tensor("b_lsum", [128, 1], F32))
            rl = p2.enter_context(nc.sbuf_tensor("b_rl", [128, 1], F32))
            ws = WStream(kb, p2, "b_w2", D)
            kb.wait("sync", cc, cc.n)
            kb.wait("gpsimd", cc, cc.n)
            with ExitStack() as pq:
                psum_q = pq.enter_context(nc.psum_tensor("b_pq", [128, TOK], F32))
                for m in range(DC):
                    ws.load(m, st.d["l1_wq_t"].ap()[m])
                    wsl = ws.use(m)
                    kb.wait("tensor", ae, ae.n)
                    for nb in range(2):
                        for kc in range(DC):
                            kb.op("tensor", lambda h, o=psum_q[:, nb * 512:(nb + 1) * 512],
                                         l=wsl[:, kc * 128:(kc + 1) * 128],
                                         r=xn[:, kc * TOK + nb * 512: kc * TOK + (nb + 1) * 512],
                                         s0=(kc == 0), s1=(kc == DC - 1):
                                      h.matmul(o, l, r, start=s0, stop=s1))
                    ws.release(m)
                    t_pq = pe.n
                    kb.op("scalar", lambda h, d=qg[:, m * TOK:(m + 1) * TOK], s=psum_q[:, :]:
                          h.activation(d, s, AF.Identity, scale=scale), wait=[(pe, t_pq)], inc=ae)
            t_qdone = pe.n
            ktb = [xn[:, 0:8192], xn[:, 8192:16384]]
            ktb_done = [t_qdone, t_qdone]
            vtb = [ws.bf[0][:, 0:1024], ws.bf[0][:, 1024:2048],
                   ws.bf[1][:, 0:1024], ws.bf[1][:, 1024:2048]]
            vtb_done = [t_qdone] * 4
            with ExitStack() as pp:
                psum_s = pp.enter_context(nc.psum_tensor("b_ps", [128, SEQ], F32))
                psum_t = pp.enter_context(nc.psum_tensor("b_pt", [128, SEQ], BF16))
                psum_o = pp.enter_context(nc.psum_tensor("b_po", [128, 512], F32))
                kb.wait("tensor", ae, ae.n)
                for g in range(2):
                    for qs in range(4):
                        qt = g * 4 + qs
                        mk = mask[0]
                        kb.wait("sync", ve, ve.n)
                        t_mk = kb.dma("sync", mk[:, :], st.d["l1_mask"].ap()[qt])
                        kb.wait("tensor", ae, ae.n)
                        for nb in range(4):
                            r = nb // 2
                            s = (qt * 4 + nb) % 2
                            kb.wait("sync", pe, ktb_done[s])
                            t_ld = kb.dma("sync",
                                          ktb[s].rearrange("p (c n) -> p c n", c=DC),
                                          kvg.ap()[r * 6144: r * 6144 + 2048,
                                                   (nb % 2) * 512:(nb % 2 + 1) * 512]
                                          .rearrange("(c p) t -> p c t", p=128))
                            kb.wait("tensor", ld, t_ld)
                            for kc in range(DC):
                                kb.op("tensor", lambda h, o=psum_s[:, nb * 512:(nb + 1) * 512],
                                             l=qg[:, kc * TOK + g * 512 + qs * 128:
                                                  kc * TOK + g * 512 + (qs + 1) * 128],
                                             r=ktb[s][:, kc * 512:(kc + 1) * 512],
                                             s0=(kc == 0), s1=(kc == DC - 1):
                                          h.matmul(o, l, r, start=s0, stop=s1))
                            kb.mark("tensor", pe)
                            ktb_done[s] = pe.n
                        t_sc = pe.n
                        kb.op("vector", lambda h, d=psum_s[:, :], a=psum_s[:, :], b=mk[:, :]:
                              h.tensor_add(d, a, b),
                              wait=[(pe, t_sc), (kb.sem("ld"), t_mk)], inc=ve)
                        t_mskd = ve.n
                        kb.op("vector", lambda h, d=nmax[:, :], s=psum_s[:, :]:
                              h.reduce_max(d, s, axis=AX.X, negate=True),
                              wait=[(ve, t_mskd)], inc=ve)
                        t_mx = ve.n
                        kb.op("scalar", lambda h, d=sexp[:, :], s=psum_s[:, :],
                                     b=nmax[:, :], a=lsum[:, :]:
                              h.activation(d, s, AF.Exp, bias=b, accum_out=a),
                              wait=[(ve, t_mx)], inc=ae)
                        t_ex = ae.n
                        kb.op("vector", lambda h, d=rl[:, :], s=lsum[:, :]:
                              h.reciprocal(d, s), wait=[(ae, t_ex)], inc=ve)
                        t_rl = ve.n
                        kb.op("vector", lambda h, d=sexp[:, :], s=sexp[:, :], r=rl[:, :]:
                              h.tensor_scalar_mul(d, s, r), wait=[(ve, t_rl)], inc=ve)
                        t_nrm = ve.n
                        kb.wait("tensor", ve, t_nrm)
                        kb.wait("tensor", ae, ae.n)
                        for kc in range(DC):
                            kb.op("tensor", lambda h, o=psum_t[:, kc * 128:(kc + 1) * 128],
                                         i=sexp[:, kc * 128:(kc + 1) * 128], d=st.id_bf[:, :]:
                                      h.transpose(o, i, d))
                        kb.mark("tensor", pe)
                        t_tr = pe.n
                        kb.op("scalar", lambda h,
                                     d=atn[:, :].rearrange("p (c n) -> p c n", c=DC)[:, :, qs * 128:(qs + 1) * 128],
                                     s=psum_t[:, :].rearrange("p (c n) -> p c n", c=DC):
                              h.copy(d, s), wait=[(pe, t_tr)], inc=ae)
                    t_stripes = ae.n
                    kb.wait("tensor", ae, t_stripes)
                    for m in range(DC):
                        kb.wait("tensor", ae, ae.n)
                        for r in range(2):
                            j = (m * 2 + r) % 4
                            base = r * 6144 + 2048
                            kb.wait("sync", pe, vtb_done[j])
                            t_ld = kb.dma("sync",
                                          vtb[j][:, :].rearrange("p (c n) -> p c n", c=8),
                                          kvg.ap()[base: base + 2048, :]
                                          .rearrange("(t two) c -> t (two c)", two=2)
                                          [:, m * 128:(m + 1) * 128]
                                          .rearrange("(kc p) h -> p kc h", p=128))
                            kb.wait("tensor", ld, t_ld)
                            for kc8 in range(8):
                                kc = r * 8 + kc8
                                kb.op("tensor", lambda h, o=psum_o[:, :],
                                             l=vtb[j][:, kc8 * 128:(kc8 + 1) * 128],
                                             r2=atn[:, kc * 512:(kc + 1) * 512],
                                             s0=(kc == 0), s1=(kc == DC - 1):
                                          h.matmul(o, l, r2, start=s0, stop=s1))
                            kb.mark("tensor", pe)
                            vtb_done[j] = pe.n
                        t_av = pe.n
                        s_ = m % 2
                        kb.op("scalar", lambda h, d=afb[s_][:, :], s=psum_o[:, :]:
                              h.copy(d, s), wait=[(pe, t_av), (kb.sem("gd"), afb_done[s_])],
                              inc=ae)
                        t_afb = ae.n
                        kb.wait("gpsimd", ae, t_afb)
                        kb.dma("gpsimd", afd.ap()[m * 128:(m + 1) * 128, g * 512:(g + 1) * 512],
                               afb[s_][:, :], sem=gd)
                        afb_done[s_] = gd.n
            kb.barrier("l1attn_p2")
        if DBG_STOP == "l1_p2":
            return
        with ExitStack() as p3:
            ws = WStream(kb, p3, "b_w3", D)
            af = p3.enter_context(nc.sbuf_tensor("b_af", [128, DC * TOK], BF16))
            kb.wait("sync", gd, gd.n)
            t_af = kb.dma("sync", af[:, :].rearrange("p (c t) -> p c t", c=DC),
                          afd.ap().rearrange("(c p) t -> p c t", p=128))
            kb.wait("tensor", ld, t_af)
            with ExitStack() as pp:
                psum_op = pp.enter_context(nc.psum_tensor("b_pop", [128, TOK], F32))
                for m in range(DC):
                    ws.load(m, st.d["l1_wo_t"].ap()[m])
                    wsl = ws.use(m)
                    kb.wait("tensor", ve, ve.n)
                    for nb in range(2):
                        for kc in range(DC):
                            kb.op("tensor", lambda h, o=psum_op[:, nb * 512:(nb + 1) * 512],
                                         l=wsl[:, kc * 128:(kc + 1) * 128],
                                         r=af[:, kc * TOK + nb * 512: kc * TOK + (nb + 1) * 512],
                                         s0=(kc == 0), s1=(kc == DC - 1):
                                      h.matmul(o, l, r, start=s0, stop=s1))
                    ws.release(m)
                    t_pop = pe.n
                    kb.op("vector", lambda h, d=st.xs[:, m * TOK:(m + 1) * TOK],
                                 s=st.xs[:, m * TOK:(m + 1) * TOK], o=psum_op[:, :]:
                          h.scalar_tensor_tensor(d, s, 2.0, o, ALU.mult, ALU.add),
                          wait=[(pe, t_pop)], inc=ve)
            kb.barrier("l1attn_p3")
    emit_postln_stream(kb, st, eps_col=C_EPS5)



def emit_absmax_clamp(kb, st, px, xn_bf, ndc, c0, W):
    nc = st.nc
    pe, ve, ae = kb.sem("pe"), kb.sem("ve"), kb.sem("ae")
    st._uid[0] += 1
    am = f"am{st._uid[0]}"
    mx = px.enter_context(nc.sbuf_tensor(am + "_mx", [128, W], BF16))
    mn = px.enter_context(nc.sbuf_tensor(am + "_mn", [128, W], BF16))
    grow = px.enter_context(nc.sbuf_tensor(am + "_grow", [1, W], F32))
    rrow = px.enter_context(nc.sbuf_tensor(am + "_rrow", [1, W], F32))
    sb = px.enter_context(nc.sbuf_tensor(am + "_sb", [128, W], F32))
    gb = px.enter_context(nc.sbuf_tensor(am + "_gb", [128, W], F32))
    hi = px.enter_context(nc.sbuf_tensor(am + "_hi", [128, 1], F32))
    lo = px.enter_context(nc.sbuf_tensor(am + "_lo", [128, 1], F32))
    u = px.enter_context(nc.sbuf_tensor(am + "_u", [128, W], F32))
    with ExitStack() as pp:
        ptr = pp.enter_context(nc.psum_tensor(am + "_ptr", [128, 128], BF16))
        prow = [pp.enter_context(nc.psum_tensor(am + f"_prow{j}", [1, 128], F32))
                for j in range(W // 128)]
        pbc = pp.enter_context(nc.psum_tensor(am + "_pbc", [128, W], F32))
        kb.op("vector", lambda h, d=mx[:, :], a=xn_bf(0), b=xn_bf(1):
              h.tensor_tensor(d, a, b, ALU.max), inc=ve)
        kb.op("vector", lambda h, d=mn[:, :], a=xn_bf(0), b=xn_bf(1):
              h.tensor_tensor(d, a, b, ALU.min), inc=ve)
        for kc in range(2, ndc):
            kb.op("vector", lambda h, d=mx[:, :], a=mx[:, :], b=xn_bf(kc):
                  h.tensor_tensor(d, a, b, ALU.max), inc=ve)
            kb.op("vector", lambda h, d=mn[:, :], a=mn[:, :], b=xn_bf(kc):
                  h.tensor_tensor(d, a, b, ALU.min), inc=ve)
        kb.op("vector", lambda h, d=mx[:, :], a=mn[:, :], b=mx[:, :]:
              h.scalar_tensor_tensor(d, a, -1.0, b, ALU.mult, ALU.max), inc=ve)
        if AM_STOP == "chain":
            kb.barrier("am")
            return
        nj = W // 128
        t_mx = ve.n
        kb.wait("tensor", ve, t_mx)
        for j in range(nj):
            kb.op("tensor", lambda h, o=ptr[:, :],
                         i=mx[:, j * 128:(j + 1) * 128], d=st.id_bf[:, :]:
                      h.transpose(o, i, d), wait=[(ve, ve.n)], inc=pe)
            t_tr = pe.n
            kb.op("vector", lambda h, d=st.stats[:, j:j + 1], s=ptr[:, :]:
                  h.reduce_max(d, s, axis=AX.X), wait=[(pe, t_tr)], inc=ve)
        t_gtm = ve.n
        if AM_STOP == "tr":
            kb.barrier("am")
            return
        kb.wait("tensor", ve, t_gtm)
        for j in range(nj):
            kb.op("tensor", lambda h, o=prow[j][:, :],
                         i=st.stats[:, j:j + 1], d=st.id_f32[:, :]:
                      h.transpose(o, i, d))
        kb.mark("tensor", pe)
        t_row = pe.n
        for j in range(nj):
            kb.op("scalar", lambda h, d=grow[:, j * 128:(j + 1) * 128], s=prow[j][:, :]:
                  h.copy(d, s), wait=[(pe, t_row)] if j == 0 else [], inc=ae)
        t_grow = ae.n
        kb.op("vector", lambda h, d=rrow[:, :], s=grow[:, :]:
              h.reciprocal(d, s), wait=[(ae, t_grow)], inc=ve)
        t_rows = (ae.n, ve.n)
        if AM_STOP == "row":
            kb.barrier("am")
            return
        kb.op("tensor", lambda h, o=pbc[:, :], l=st.ones_f32[0:1, :], r=rrow[:, :]:
              h.matmul(o, l, r, start=True, stop=True),
              wait=[(ve, t_rows[1])], inc=pe)
        t_b1 = pe.n
        kb.op("scalar", lambda h, d=sb[:, :], s=pbc[:, :]:
              h.activation(d, s, AF.Identity, scale=128.0), wait=[(pe, t_b1)], inc=ae)
        t_sb = ae.n
        kb.op("tensor", lambda h, o=pbc[:, :], l=st.ones_f32[0:1, :], r=grow[:, :]:
              h.matmul(o, l, r, start=True, stop=True),
              wait=[(ae, max(t_rows[0], t_sb))], inc=pe)
        t_b2 = pe.n
        kb.op("scalar", lambda h, d=gb[:, :], s=pbc[:, :]:
              h.activation(d, s, AF.Identity, scale=1.0 / 128.0), wait=[(pe, t_b2)], inc=ae)
        kb.op("scalar", lambda h, d=hi[:, :], s=st.consts[:, C_EPS6:C_EPS6 + 1],
                     b=st.consts[:, C_128:C_128 + 1]:
              h.activation(d, s, AF.Identity, bias=b, scale=-1.0), inc=ae)
        kb.op("scalar", lambda h, d=lo[:, :], s=st.consts[:, C_EPS6:C_EPS6 + 1],
                     b=st.consts[:, C_N128:C_N128 + 1]:
              h.activation(d, s, AF.Identity, bias=b, scale=1.0), inc=ae)
        t_all = ae.n
        if AM_STOP == "bc":
            kb.barrier("am")
            return
        for kc in range(ndc):
            kb.op("vector", lambda h, d=u[:, :], a=xn_bf(kc), b=sb[:, :]:
                  h.tensor_mul(d, a, b), wait=[(ae, t_all)] if kc == 0 else [], inc=ve)
            kb.op("vector", lambda h, d=u[:, :], s=u[:, :], s1=hi[:, :], s2=lo[:, :]:
                  h.tensor_scalar(d, s, s1, s2, ALU.min, ALU.max), inc=ve)
            kb.op("vector", lambda h, d=xn_bf(kc), a=u[:, :], b=gb[:, :]:
                  h.tensor_mul(d, a, b), inc=ve)
    kb.barrier("absmax")


def _tern_gamma_pass(kb, st, px):
    nc = st.nc
    ld, pe, ve, ae = (kb.sem(x) for x in ("ld", "pe", "ve", "ae"))
    gacc = px.enter_context(nc.sbuf_tensor("t_gacc", [128, D4C + DC], F32))
    gcol = px.enter_context(nc.sbuf_tensor("t_gcol", [128, 2], F32))
    st.gfc = px.enter_context(nc.sbuf_tensor("t_gfc", [128, 1], F32))
    st.gpr = px.enter_context(nc.sbuf_tensor("t_gpr", [128, 1], F32))
    st.thrf = px.enter_context(nc.sbuf_tensor("t_thrf", [128, 1], F32))
    st.nthrf = px.enter_context(nc.sbuf_tensor("t_nthrf", [128, 1], F32))
    st.thrp = px.enter_context(nc.sbuf_tensor("t_thrp", [128, 1], F32))
    st.nthrp = px.enter_context(nc.sbuf_tensor("t_nthrp", [128, 1], F32))
    with ExitStack() as sx:
        wstage = [sx.enter_context(nc.sbuf_tensor(f"t_ws{i}", [128, D4], F32)) for i in range(2)]
        rel = [0, 0]
        def feed(idx, col, dram_ap, width):
            s = idx % 2
            kb.wait("sync", ve, rel[s])
            t = kb.dma("sync", wstage[s][:, 0:width], dram_ap)
            kb.op("vector", lambda h, d=gacc[:, col:col + 1], x=wstage[s][:, 0:width]:
                  h.reduce_sum(d, x, axis=AX.X, apply_absolute_value=True),
                  wait=[(ld, t)], inc=ve)
            rel[s] = ve.n
        idx = 0
        for m in range(D4C):
            feed(idx, m, st.d["l1_wfc_t"].ap()[m], D); idx += 1
        for m in range(DC):
            feed(idx, D4C + m, st.d["l1_wpr_t"].ap()[m], D4); idx += 1
        kb.op("vector", lambda h, d=gcol[:, 0:1], s=gacc[:, 0:D4C]:
              h.reduce_sum(d, s, axis=AX.X), inc=ve)
        kb.op("vector", lambda h, d=gcol[:, 1:2], s=gacc[:, D4C:D4C + DC]:
              h.reduce_sum(d, s, axis=AX.X), inc=ve)
        t_gc = ve.n
        with ExitStack() as pp:
            pg = pp.enter_context(nc.psum_tensor("t_pg", [128, 2], F32))
            kb.op("tensor", lambda h, o=pg[:, :], l=st.ones_f32[:, :], r=gcol[:, :]:
                  h.matmul(o, l, r, start=True, stop=True), wait=[(ve, t_gc)], inc=pe)
            t_pg = pe.n
            inv_n = 1.0 / (D4 * D)
            kb.op("scalar", lambda h, d=st.gfc[:, :], s=pg[:, 0:1]:
                  h.activation(d, s, AF.Identity, scale=inv_n), wait=[(pe, t_pg)], inc=ae)
            kb.op("scalar", lambda h, d=st.gpr[:, :], s=pg[:, 1:2]:
                  h.activation(d, s, AF.Identity, scale=inv_n), inc=ae)
            for gsrc, tdst, ndst in ((st.gfc, st.thrf, st.nthrf), (st.gpr, st.thrp, st.nthrp)):
                kb.op("scalar", lambda h, d=tdst[:, :], s=gsrc[:, :],
                             b=st.consts[:, C_HEPS:C_HEPS + 1]:
                      h.activation(d, s, AF.Identity, bias=b, scale=0.5), inc=ae)
                kb.op("scalar", lambda h, d=ndst[:, :], s=gsrc[:, :],
                             b=st.consts[:, C_NHEPS:C_NHEPS + 1]:
                      h.activation(d, s, AF.Identity, bias=b, scale=-0.5), inc=ae)
    kb.barrier("terngamma")


def emit_l1ffn(kb, st):
    nc = st.nc
    ld, pe, ve, ae = (kb.sem(x) for x in ("ld", "pe", "ve", "ae"))
    with ExitStack() as gx:
        _tern_gamma_pass(kb, st, gx)
        if DBG_STOP == "l1f_gamma":
            return
        for half in range(2):
            c0 = half * 512
            W = 512
            with ExitStack() as sx:
                hbf = sx.enter_context(nc.sbuf_tensor("c_h" + str(half), [128, D4C * W], BF16))
                sxa = sx.enter_context(ExitStack())
                xnf = sxa.enter_context(nc.sbuf_tensor("c_xn" + str(half), [128, DC * W], BF16))
                with ExitStack() as px:
                    emit_ln_to_bf(kb, st, px, DC, W,
                                  dst_bf=lambda kc: xnf[:, kc * W:(kc + 1) * W],
                                  eps_col=C_EPS6, inv=1.0 / D,
                                  f32_src=lambda kc: st.xs[:, kc * TOK + c0: kc * TOK + c0 + W])
                if DBG_STOP == "l1f_ln":
                    return
                with ExitStack() as px:
                    emit_absmax_clamp(kb, st, px,
                                      lambda kc: xnf[:, kc * W:(kc + 1) * W], DC, c0, W)
                if DBG_STOP == "l1f_am":
                    return
                with ExitStack() as pp:
                    psum_f = [pp.enter_context(nc.psum_tensor(f"c_pf{half}{i}", [128, W], F32))
                              for i in range(2)]
                    tsf = TernStream(kb, pp, st, "c_tsf" + str(half), D, st.thrf, st.nthrf)
                    tsf.load(0, st.d["l1_wfc_t"].ap()[0])
                    gelu_done = [0, 0]
                    for oc in range(D4C):
                        if oc + 1 < D4C:
                            tsf.load(oc + 1, st.d["l1_wfc_t"].ap()[oc + 1])
                        wsl = tsf.use(oc)
                        ps = oc % 2
                        kb.wait("tensor", ae, gelu_done[ps])
                        for kc in range(DC):
                            kb.op("tensor", lambda h, o=psum_f[ps][:, :],
                                         l=wsl[:, kc * 128:(kc + 1) * 128],
                                         r=xnf[:, kc * W:(kc + 1) * W],
                                         s0=(kc == 0), s1=(kc == DC - 1):
                                      h.matmul(o, l, r, start=s0, stop=s1))
                        tsf.release(oc)
                        t_pf = pe.n
                        kb.op("scalar", lambda h, d=hbf[:, oc * W:(oc + 1) * W],
                                     s=psum_f[ps][:, :], g=st.gfc[:, :]:
                              h.activation(d, s, AF.Gelu, scale=g), wait=[(pe, t_pf)], inc=ae)
                        gelu_done[ps] = ae.n
                kb.barrier("l1ffn_fc")
                sxa.close()
                if DBG_STOP == "l1f_fc":
                    return
                with ExitStack() as px:
                    emit_ln_to_bf(kb, st, px, D4C, W,
                                  dst_bf=lambda kc: hbf[:, kc * W:(kc + 1) * W],
                                  eps_col=C_EPS6, inv=1.0 / D4,
                                  bf_src=lambda kc: hbf[:, kc * W:(kc + 1) * W])
                kb.barrier("l1ffn_hln")
                if DBG_STOP == "l1f_hln":
                    return
                with ExitStack() as px:
                    emit_absmax_clamp(kb, st, px,
                                      lambda kc: hbf[:, kc * W:(kc + 1) * W], D4C, c0, W)
                if DBG_STOP == "l1f_am2":
                    return
                with ExitStack() as pp:
                    tsp = TernStream(kb, pp, st, "c_tsp" + str(half), D, st.thrp, st.nthrp)
                    psum_p = [pp.enter_context(nc.psum_tensor(f"c_pp{half}{i}", [128, W], F32))
                              for i in range(2)]
                    evac_done = [0, 0]

                    def pr_slab(i):
                        return st.d["l1_wpr_t"].ap()[i // 4][:, (i % 4) * D:(i % 4 + 1) * D]

                    tsp.load(0, pr_slab(0))
                    for oc in range(DC):
                        ps = oc % 2
                        kb.wait("tensor", ve, evac_done[ps])
                        for qtr in range(4):
                            i = oc * 4 + qtr
                            if i + 1 < DC * 4:
                                tsp.load(i + 1, pr_slab(i + 1))
                            wsl = tsp.use(i)
                            for kc in range(DC):
                                kk = qtr * DC + kc
                                kb.op("tensor", lambda h, o=psum_p[ps][:, :],
                                             l=wsl[:, kc * 128:(kc + 1) * 128],
                                             r=hbf[:, kk * W:(kk + 1) * W],
                                             s0=(kk == 0), s1=(kk == D4C - 1):
                                          h.matmul(o, l, r, start=s0, stop=s1))
                            tsp.release(i)
                        t_pp = pe.n
                        kb.op("vector", lambda h, d=st.xs[:, oc * TOK + c0: oc * TOK + c0 + W],
                                     s=psum_p[ps][:, :], g=st.gpr[:, :]:
                              h.scalar_tensor_tensor(d, s, g, d, ALU.mult, ALU.add),
                              wait=[(pe, t_pp)], inc=ve)
                        evac_done[ps] = ve.n
                kb.barrier("l1ffn_proj")
                if DBG_STOP == "l1f_proj":
                    return
    emit_postln_stream(kb, st, eps_col=C_EPS5)


STAGE_EMITTERS = {
    "l0attn": emit_l0attn,
    "l0ffn": emit_l0ffn,
    "l1attn": emit_l1attn,
    "l1ffn": emit_l1ffn,
}

STAGE_INPUTS = {
    "l0attn": ["l0_wq_t", "l0_wk_t", "l0_wv", "l0_bq", "l0_bk", "l0_bv_row",
               "l0_wo_t", "l0_bo"],
    "l0ffn": ["l0_wg_t", "l0_wu_t", "l0_wd_t", "l0_gs", "l0_us", "l0_ds"],
    "l1attn": ["l1_wq_t", "l1_wk_t", "l1_wv", "l1_wo_t", "l1_gamma", "l1_beta",
               "l1_mask"],
    "l1ffn": ["l1_wfc_t", "l1_wpr_t"],
}


HPERM = []
_p0 = [h for h in range(NQ) if (h // 4) % 2 == 0]
_p1 = [h for h in range(NQ) if (h // 4) % 2 == 1]
for _c in range(16):
    HPERM += [_p0[_c], _p1[_c]]


def _host_arrays(inputs):
    a = {}
    wq = np.asarray(inputs["L0_attn_Wqkv"][:, :2048]).reshape(D, NQ, HD)
    a["l0_wq_t"] = _formB(np.ascontiguousarray(wq[:, HPERM, :].reshape(D, 2048)))
    a["l0_wk_t"] = _formB(inputs["L0_attn_Wqkv"][:, 2048:2560])
    a["l0_wv"] = np.ascontiguousarray(inputs["L0_attn_Wqkv"][:, 2560:])
    bq = np.asarray(inputs["L0_attn_bqkv"][:2048]).reshape(NQ, HD)
    a["l0_bq"] = _cols128(np.ascontiguousarray(bq[HPERM].reshape(2048)))
    a["l0_bk"] = _cols128(inputs["L0_attn_bqkv"][2048:2560])
    a["l0_bv_row"] = np.ascontiguousarray(inputs["L0_attn_bqkv"][2560:]).reshape(1, 512)
    wo = np.asarray(inputs["L0_attn_Wo"]).reshape(NQ, HD, D)
    a["l0_wo_t"] = _formB(np.ascontiguousarray(wo[HPERM].reshape(D, D)))
    a["l0_bo"] = _cols128(inputs["L0_attn_bo"])
    a["l0_wg_t"] = _formB(np.ascontiguousarray(inputs["L0_ffn_gate_w"].T))
    a["l0_wu_t"] = _formB(np.ascontiguousarray(inputs["L0_ffn_up_w"].T))
    a["l0_wd_t"] = _formB(np.ascontiguousarray(inputs["L0_ffn_down_w"].T))
    a["l0_gs"] = _cols128(inputs["L0_ffn_gate_s"])
    a["l0_us"] = _cols128(inputs["L0_ffn_up_s"])
    a["l0_ds"] = _cols128(inputs["L0_ffn_down_s"])
    a["l1_wq_t"] = _formB(inputs["L1_attn_qkv"][:, :2048])
    a["l1_wk_t"] = _formB(inputs["L1_attn_qkv"][:, 2048:4096])
    a["l1_wv"] = np.ascontiguousarray(inputs["L1_attn_qkv"][:, 4096:])
    a["l1_wo_t"] = _formB(inputs["L1_attn_o_proj"])
    a["l1_gamma"] = _cols128(inputs["L1_attn_gamma"])
    a["l1_beta"] = _cols128(inputs["L1_attn_beta"])
    a["l1_wfc_t"] = _formB(np.ascontiguousarray(inputs["L1_ffn_w_fc"].T))
    a["l1_wpr_t"] = _formB(np.ascontiguousarray(inputs["L1_ffn_w_proj"].T))
    return a


def _const_arrays():
    c = {}
    consts = np.zeros((128, 8), np.float32)
    consts[:, C_EPS5] = 1e-5
    consts[:, C_EPS6] = 1e-6
    consts[:, C_128] = 128.0
    consts[:, C_N128] = -128.0
    consts[:, C_HEPS] = 0.5e-6
    consts[:, C_NHEPS] = -0.5e-6
    c["consts"] = consts
    c["ones_bf"] = np.ones((128, 128), ml_dtypes.bfloat16)
    c["ones_f32"] = np.ones((128, 128), np.float32)
    c["id_bf"] = np.eye(128, dtype=ml_dtypes.bfloat16)
    c["id_f32"] = np.eye(128, dtype=np.float32)
    return c


def _l1_masks():
    out = []
    kidx = np.arange(SEQ)[None, :]
    for parity in range(2):
        m = np.zeros((8, 128, SEQ), np.float32)
        for j in range(8):
            q = parity * 1024 + j * 128 + np.arange(128)[:, None]
            m[j] = np.where(kidx <= q, 0.0, NEG)
        out.append(m.astype(ml_dtypes.bfloat16))
    return out


_DT = {np.dtype(np.float32): F32, np.dtype(ml_dtypes.bfloat16): BF16}


def _build_program(stages, host_shapes):
    nc = bass.Bass("TRN2", target_bir_lowering=False, debug=False,
                   detect_race_conditions=RACE_DETECT)
    ctx = ExitStack()
    st = St()
    decls = [("x_in", (D, TOK), F32), ("consts", (128, 8), F32),
             ("ones_bf", (128, 128), BF16), ("ones_f32", (128, 128), F32),
             ("id_bf", (128, 128), BF16), ("id_f32", (128, 128), F32)]
    for s in stages:
        for name in STAGE_INPUTS[s]:
            shp, dt = host_shapes[name]
            decls.append((name, shp, dt))
    alloc_persistent(nc, ctx, st, decls)
    st.d["x_out"] = nc.dram_tensor("x_out", [D, TOK], F32, kind="ExternalOutput")
    kb = KB(nc, ctx)
    kb.cur_tag = "prologue"
    emit_prologue(kb, st)
    for s in stages:
        kb.cur_tag = s
        STAGE_EMITTERS[s](kb, st)
    kb.cur_tag = "epilogue"
    emit_epilogue(kb, st)
    with nc.Block() as block:
        kb.finalize(block)
    ctx.close()
    return nc


def _run(inputs, launch_groups):
    host = _host_arrays(inputs)
    consts = _const_arrays()
    masks = _l1_masks()
    x = np.asarray(inputs["L0_attn_x"])
    xcore = []
    for c in range(NCORES):
        seq, half = c // 2, c % 2
        xcore.append(np.ascontiguousarray(x[seq, half * TOK:(half + 1) * TOK, :].T))
    host_shapes = {}
    for k, v in host.items():
        host_shapes[k] = (v.shape, _DT[v.dtype])
    host_shapes["l1_mask"] = ((8, 128, SEQ), BF16)
    for grp in launch_groups:
        nc = _build_program(grp, host_shapes)
        in_maps = []
        for c in range(NCORES):
            m = {"x_in": xcore[c]}
            m.update(consts)
            for s in grp:
                for name in STAGE_INPUTS[s]:
                    m[name] = masks[c % 2] if name == "l1_mask" else host[name]
            in_maps.append(m)
        res = run_bass_kernel_spmd(nc, in_maps, core_ids=list(range(NCORES)))
        xcore = [np.ascontiguousarray(res.results[c]["x_out"]) for c in range(NCORES)]
    out = np.empty((4, SEQ, D), np.float32)
    for c in range(NCORES):
        seq, half = c // 2, c % 2
        out[seq, half * TOK:(half + 1) * TOK, :] = xcore[c].T
    return out


def kernel(**inputs):
    return _run(inputs, LAUNCH_GROUPS)
